# version 15
# baseline (speedup 1.0000x reference)
import math
from contextlib import ExitStack

import numpy as np
import ml_dtypes

import concourse.bass as bass
import concourse.mybir as mybir
from concourse.bass_utils import run_bass_kernel_spmd

F32 = mybir.dt.float32
BF16 = mybir.dt.bfloat16
AF = mybir.ActivationFunctionType
ALU = mybir.AluOpType
AX = mybir.AxisListType

B, S, D = 16, 2048, 1024
NCORE = 8
BPC = B // NCORE
H, HD = 16, 64
E, HID = 8, 4096
NIC = D // 128
NTH = 2
TH = S // NTH
HB = 1024
NHB = HID // HB
HCB = HB // 128
TB = 512
NTB = TH // TB

ENGS = ["tensor", "scalar", "vector", "gpsimd", "sync"]


def _slopes(n):
    start = 2.0 ** (-(2.0 ** (-(math.log2(n) - 3))))
    return np.array([start * start ** i for i in range(n)], dtype=np.float32)


class Plan:

    def __init__(self):
        self.ops = {e: [] for e in ENGS}
        self.cnt = {}
        self.emitted = {e: {} for e in ENGS}
        self.gate = {}

    def C(self, sem):
        return self.cnt.get(sem, 0)

    def I(self, eng, method, *args, inc=None, by=1, **kw):
        self.ops[eng].append(("i", method, args, kw, inc, by))
        if inc:
            self.cnt[inc] = self.cnt.get(inc, 0) + by
            return (inc, self.cnt[inc], eng, by >= 16)
        return None

    def W(self, eng, sem, val):
        if val <= 0:
            return
        if self.emitted[eng].get(sem, -1) >= val:
            return
        self.emitted[eng][sem] = val
        self.ops[eng].append(("w", sem, val))

    def need(self, eng, tok):
        if tok is None:
            return
        sem, val, src_eng, is_dma = tok
        if (not is_dma) and src_eng == eng:
            return
        self.W(eng, sem, val)

    def freed(self, key, tok):
        self.gate[key] = tok

    def acquire(self, eng, key):
        self.need(eng, self.gate.get(key))


class _StopPlan(Exception):
    pass


def build_core_program(upto=None):
    nc = bass.Bass("TRN2", target_bir_lowering=False)

    d_x = nc.dram_tensor("x", [BPC, S, D], F32, kind="ExternalInput").ap()
    d_out = nc.dram_tensor("out", [BPC, S, D], F32, kind="ExternalOutput").ap()
    d_wqT = nc.dram_tensor("wqT", [D, D], F32, kind="ExternalInput").ap()
    d_wkT = nc.dram_tensor("wkT", [D, D], F32, kind="ExternalInput").ap()
    d_wvT = nc.dram_tensor("wvT", [D, D], F32, kind="ExternalInput").ap()
    d_woT = nc.dram_tensor("woT", [D, D], F32, kind="ExternalInput").ap()
    d_w1T = nc.dram_tensor("w1T", [E, D, HID], F32, kind="ExternalInput").ap()
    d_w2T = nc.dram_tensor("w2T", [E, HID, D], F32, kind="ExternalInput").ap()
    d_rwT = nc.dram_tensor("rwT", [D, E], F32, kind="ExternalInput").ap()
    d_bqpt = nc.dram_tensor("bqpt", [128, NIC], F32, kind="ExternalInput").ap()
    d_bkpt = nc.dram_tensor("bkpt", [128, NIC], F32, kind="ExternalInput").ap()
    d_bopt = nc.dram_tensor("bopt", [128, NIC], F32, kind="ExternalInput").ap()
    d_bv = nc.dram_tensor("bvr", [1, D], F32, kind="ExternalInput").ap()
    d_b1pt = nc.dram_tensor("b1pt", [128, E, HID // 128], F32, kind="ExternalInput").ap()
    d_b2 = nc.dram_tensor("b2", [E, D], F32, kind="ExternalInput").ap()
    d_identb = nc.dram_tensor("identb", [128, 128], BF16, kind="ExternalInput").ap()
    d_identf = nc.dram_tensor("identf", [128, 128], F32, kind="ExternalInput").ap()
    d_rowshift = nc.dram_tensor("rowshift", [128, H * (S // 128)], F32, kind="ExternalInput").ap()
    d_minf = nc.dram_tensor("minf", [128, 128], F32, kind="ExternalInput").ap()
    d_tq = nc.dram_tensor("tqrow", [1, S], F32, kind="ExternalInput").ap()

    es = ExitStack()
    sb = lambda name, shape, dt: es.enter_context(nc.sbuf_tensor("s_" + name, shape, dt))
    psal = lambda name, shape, dt: es.enter_context(nc.psum_tensor(name, shape, dt))
    mksem = lambda name: es.enter_context(nc.semaphore(name))

    with es:
        identb = sb("identb", [128, 128], BF16)
        identf = sb("identf", [128, 128], F32)
        rowshift = sb("rowshift", [128, H * (S // 128)], F32)
        minf = sb("minf", [128, 128], F32)
        bqpt = sb("bqpt", [128, NIC], F32)
        bkpt = sb("bkpt", [128, NIC], F32)
        bopt = sb("bopt", [128, NIC], F32)
        bq8 = sb("bq8", [128, NIC], F32)
        b1pt = sb("b1pt", [128, E, HID // 128], F32)
        b2bf = sb("b2bf", [E, D], BF16)
        rwT = sb("rwT", [128, NIC, E], BF16)
        bvrow = sb("bvrow", [1, D], BF16)
        tqbf = sb("tqbf", [1, S], BF16)
        onesb = sb("onesb", [1, 128], BF16)
        onesf = sb("onesf", [1, 128], F32)
        csh = sb("csh", [1, 512], BF16)
        epsv = sb("epsv", [128, 1], F32)
        xn_fm = sb("xn_fm", [128, NIC, S], BF16)
        out_acc = sb("out_acc", [128, NIC, TH], F32)
        cw_fm = sb("cw_fm", [E, TH], BF16)

        AR_N = 63744
        arena = sb("arena", [128, AR_N], BF16)
        arb = arena.ap()
        arf = arena.ap().bitcast(F32)

        def bfv(off, shape):
            n = int(np.prod(shape))
            apx = arb[:, off:off + n]
            if len(shape) == 1:
                return apx
            if len(shape) == 2:
                return apx.rearrange("p (a b) -> p a b", b=shape[1])
            return apx.rearrange("p (a b c) -> p a b c", b=shape[1], c=shape[2])

        def f32v(off_bf, shape):
            off = off_bf // 2
            n = int(np.prod(shape))
            apx = arf[:, off:off + n]
            if len(shape) == 1:
                return apx
            return apx.rearrange("p (a b) -> p a b", b=shape[1])

        q_fm = bfv(0, (NIC, TH))
        k_fm = bfv(8192, (NIC, S))
        v_tm = bfv(24576, (S // 128, H * 65))
        wstgA = bfv(41216, (NIC, D))
        wstgB = bfv(49408, (NIC, D))
        attn_fmb = bfv(57600, (NIC, 512))
        p_t = bfv(61696, (4, 512))
        cwrow = arb[0:1, 32768:33280]
        w1A = bfv(41216, (NIC, HB))
        w2A = bfv(49408, (HCB, D))
        w1B = bfv(8192, (NIC, HB))
        w2B = bfv(16384, (HCB, D))
        w1S, w2S = [w1A, w1B], [w2A, w2B]
        h_all = bfv(24576, (HCB, 512))
        cwb = bfv(36864, (D,))
        sqbuf = f32v(45312, (D,))
        rrow = f32v(43264, (512,))
        rrowB = f32v(46336, (512,))
        out_tm = f32v(0, (2, D))
        x_stage = f32v(4096, (2, D))
        xn_tm = bfv(41216, (2, D))
        rp = f32v(44288, (E,))
        probs = f32v(44304, (E,))
        eq1 = f32v(44320, (E,))
        t1v = f32v(44336, (E,))
        pwm = f32v(44352, (E,))
        t2v = f32v(44368, (E,))
        keep = f32v(44384, (E,))
        m1 = f32v(44400, (1,))
        m2 = f32v(44408, (1,))
        den = f32v(44416, (1,))
        rd = f32v(44424, (1,))
        ssum = f32v(44432, (1,))
        rsum = f32v(44440, (1,))
        bn6 = f32v(44448, (12,))
        m2v = f32v(44480, (1,))
        varv = f32v(44488, (1,))
        mv = f32v(44496, (2,))
        rshadj = f32v(44528, (16,))
        rshadjB = f32v(44576, (16,))
        cshB = arb[0:1, 44640:45152]
        sqv = f32v(44504, (1,))
        rstd = f32v(44512, (1,))
        mrs = f32v(44520, (1,))

        pb = [psal(f"pb{i}", [128, 512], F32) for i in range(8)]
        pbb = [p.ap().bitcast(BF16) for p in pb]

        SEM_NAMES = ["ld", "ldg", "sg", "sx0", "sx1", "sw0", "sw1", "st0", "st1",
                     "scw", "pe", "sc", "dv"]
        sems = {n: mksem(n) for n in SEM_NAMES}
        S_SX, S_SW, S_ST = ["sx0", "sx1"], ["sw0", "sw1"], ["st0", "st1"]

        P = Plan()
        I, W, C = P.I, P.W, P.C
        BK = lambda i: f"bank{i}"

        def ck(name):
            if upto is not None and upto == name:
                raise _StopPlan()

        def vch(method, *args, **kw):
            tk = I("vector", method, *args, inc="dv", **kw)
            W("vector", "dv", tk[1])
            return tk

        for dst, src in [(identb, d_identb), (identf, d_identf), (rowshift, d_rowshift),
                         (minf, d_minf), (bqpt, d_bqpt), (bkpt, d_bkpt), (bopt, d_bopt),
                         (b1pt, d_b1pt)]:
            I("sync", "dma_start", out=dst[:], in_=src[:], inc="ld", by=16)
        n_const = C("ld")
        for dst, src in [(b2bf, d_b2), (bvrow, d_bv), (tqbf, d_tq)]:
            I("gpsimd", "dma_start", out=dst[:], in_=src[:], inc="ldg", by=16)
        I("gpsimd", "dma_start", out=rwT[:], in_=d_rwT.rearrange("(c p) e -> p c e", p=128),
          inc="ldg", by=16)
        n_constg = C("ldg")
        I("gpsimd", "memset", onesb[:], 1.0)
        I("gpsimd", "memset", onesf[:], 1.0, inc="sg")
        n_sg0 = C("sg")
        for e_ in ["tensor", "scalar", "vector", "gpsimd"]:
            W(e_, "ld", n_const)
            W(e_, "ldg", n_constg)
            W(e_, "sg", n_sg0)
        I("vector", "memset", epsv[:], 1e-5)
        tok_bq8 = I("vector", "tensor_scalar", out=bq8[:], in0=bqpt[:], scalar1=0.125,
                    scalar2=None, op0=ALU.mult, inc="dv")

        nx = [0, 0]
        nst = [0, 0]
        ncw = [0]
        nsw = [0, 0]

        try:
          for s in range(BPC):
            W("vector", "pe", C("pe"))
            for tc in range(S // 128):
                p = tc % 2
                P.acquire("sync", f"x_stage{p}")
                W("sync", S_SX[p], 16 * nx[p])
                nx[p] += 1
                I("sync", "dma_start", out=x_stage[:, p, :],
                  in_=d_x[s, tc * 128:(tc + 1) * 128, :], inc=S_SX[p], by=16)
                xtok = (S_SX[p], 16 * nx[p], "sync", True)
                P.need("vector", xtok)
                P.need("scalar", xtok)
                tok = vch("reduce_sum", mv[:, 0:1], x_stage[:, p, :], axis=AX.X)
                P.acquire("scalar", "sqbuf")
                stok = I("scalar", "activation", out=sqbuf, in_=x_stage[:, p, :],
                         func=AF.Square, inc="sc")
                P.need("vector", stok)
                tok2 = vch("reduce_sum", mv[:, 1:2], sqbuf, axis=AX.X)
                P.freed("sqbuf", tok2)
                vch("tensor_scalar", out=mv, in0=mv, scalar1=1.0 / D,
                    scalar2=None, op0=ALU.mult)
                vch("tensor_tensor", out=m2v, in0=mv[:, 0:1], in1=mv[:, 0:1],
                    op=ALU.mult)
                tok = vch("tensor_tensor", out=varv, in0=mv[:, 1:2], in1=m2v,
                          op=ALU.subtract)
                P.need("scalar", tok)
                tok = I("scalar", "activation", out=sqv, in_=varv, func=AF.Sqrt,
                        bias=epsv[:], scale=1.0, inc="sc")
                P.need("vector", tok)
                vch("reciprocal", rstd, sqv)
                vch("tensor_tensor", out=mrs, in0=mv[:, 0:1], in1=rstd, op=ALU.mult)
                P.acquire("vector", f"xn_tm{p}")
                tok_xn = I("vector", "tensor_scalar", out=xn_tm[:, p, :], in0=x_stage[:, p, :],
                           scalar1=rstd, scalar2=mrs, op0=ALU.mult, op1=ALU.subtract,
                           inc="dv")
                P.freed(f"x_stage{p}", tok_xn)
                P.need("tensor", tok_xn)
                for oc in range(NIC):
                    bank = 6 + oc % 2
                    P.acquire("tensor", BK(bank))
                    ttok = I("tensor", "transpose", pbb[bank][:, 0:128],
                             xn_tm[:, p, oc * 128:(oc + 1) * 128], identb[:], inc="pe")
                    P.need("vector", ttok)
                    vtok = I("vector", "tensor_copy", out=xn_fm[:, oc, tc * 128:(tc + 1) * 128],
                             in_=pbb[bank][:, 0:128], inc="dv")
                    P.freed(BK(bank), vtok)
                    if oc == NIC - 1:
                        P.freed(f"xn_tm{p}", ttok)
            xn_done = ("dv", C("dv"), "vector", False)
            P.freed("wstgA", ("pe", C("pe"), "tensor", False))
            ck(f"ln{s}")

            for th in range(NTH):
                qb = th * TH
                kv = (th + 1) * TH
                nkc = kv // 128

                def stg_fill(slot_i, wsrc):
                    P.acquire("gpsimd", "wstgA" if slot_i == 0 else "wstgB")
                    W("gpsimd", S_SW[slot_i], 16 * nsw[slot_i])
                    nsw[slot_i] += 1
                    I("gpsimd", "dma_start", out=(wstgA if slot_i == 0 else wstgB)[:, :, :],
                      in_=wsrc.rearrange("(c p) n -> p c n", p=128), inc=S_SW[slot_i], by=16)
                    return (S_SW[slot_i], 16 * nsw[slot_i], "gpsimd", True)

                tok_wq = stg_fill(0, d_wqT)
                tok_wk = stg_fill(1, d_wkT)

                W("scalar", S_SX[0], 16 * nx[0])
                W("scalar", S_SX[1], 16 * nx[1])
                W("scalar", S_ST[0], 16 * nst[0])
                W("scalar", S_ST[1], 16 * nst[1])
                W("scalar", "dv", C("dv"))

                P.need("tensor", xn_done)
                P.need("tensor", tok_wq)
                P.need("scalar", tok_bq8)
                for u, (tqb, oc) in enumerate([(a, b) for a in range(TH // 512)
                                               for b in range(NIC)]):
                    bank = u % 4
                    P.acquire("tensor", BK(bank))
                    for ic in range(NIC):
                        mm = I("tensor", "matmul", pb[bank][:],
                               wstgA[:, ic, oc * 128:(oc + 1) * 128],
                               xn_fm[:, ic, qb + tqb * 512: qb + (tqb + 1) * 512],
                               start=(ic == 0), stop=(ic == NIC - 1),
                               inc=("pe" if ic == NIC - 1 else None))
                    P.need("scalar", mm)
                    stok = I("scalar", "activation", out=q_fm[:, oc, tqb * 512:(tqb + 1) * 512],
                             in_=pb[bank][:], func=AF.Identity, bias=bq8[:, oc:oc + 1],
                             scale=0.125, inc="sc")
                    P.freed(BK(bank), stok)
                P.freed("wstgA", ("pe", C("pe"), "tensor", False))

                P.need("tensor", tok_wk)
                P.acquire("scalar", "kreg")
                for u, (kb, oc) in enumerate([(a, b) for a in range(kv // 512)
                                              for b in range(NIC)]):
                    bank = u % 4
                    P.acquire("tensor", BK(bank))
                    for ic in range(NIC):
                        mm = I("tensor", "matmul", pb[bank][:],
                               wstgB[:, ic, oc * 128:(oc + 1) * 128],
                               xn_fm[:, ic, kb * 512:(kb + 1) * 512],
                               start=(ic == 0), stop=(ic == NIC - 1),
                               inc=("pe" if ic == NIC - 1 else None))
                    P.need("scalar", mm)
                    stok = I("scalar", "activation", out=k_fm[:, oc, kb * 512:(kb + 1) * 512],
                             in_=pb[bank][:], func=AF.Identity, bias=bkpt[:, oc:oc + 1],
                             scale=1.0, inc="sc")
                    P.freed(BK(bank), stok)
                P.freed("wstgB", ("pe", C("pe"), "tensor", False))
                tok_wv = stg_fill(0, d_wvT)
                tok_wo = stg_fill(1, d_woT)

                W("gpsimd", "pe", C("pe"))
                W("gpsimd", "dv", C("dv"))
                W("gpsimd", "sc", C("sc"))
                W("gpsimd", "scw", 16 * ncw[0])
                for tkc in range(nkc):
                    ocols = v_tm[:, tkc, :].rearrange("p (h d) -> p h d", d=65)[:, :, 64:65]
                    I("gpsimd", "memset", ocols, 1.0,
                      inc=("sg" if tkc == nkc - 1 else None))
                ones_ready = ("sg", C("sg"), "gpsimd", False)

                P.need("tensor", tok_wv)
                for u, (tkc, oh) in enumerate([(a, b) for a in range(nkc) for b in range(2)]):
                    bank = u % 4
                    P.acquire("tensor", BK(bank))
                    for ic in range(NIC):
                        I("tensor", "matmul", pb[bank][:],
                          xn_fm[:, ic, tkc * 128:(tkc + 1) * 128],
                          wstgA[:, ic, oh * 512:(oh + 1) * 512],
                          start=(ic == 0), stop=False)
                    mm = I("tensor", "matmul", pb[bank][:], onesb[0:1, 0:128],
                           bvrow[0:1, oh * 512:(oh + 1) * 512], start=False, stop=True,
                           inc="pe")
                    P.need("vector", mm)
                    P.need("vector", ones_ready)
                    dst = v_tm[:, tkc, oh * 520:(oh + 1) * 520].rearrange(
                        "p (h d) -> p h d", d=65)[:, :, 0:64]
                    src = pb[bank].ap().rearrange("p (h d) -> p h d", d=64)
                    vtok = I("vector", "tensor_copy", out=dst, in_=src, inc="dv")
                    P.freed(BK(bank), vtok)
                P.freed("wstgA", ("pe", C("pe"), "tensor", False))

                for tc in range(TH // 128):
                    P.acquire("tensor", BK(6))
                    for ic in range(NIC):
                        mm = I("tensor", "matmul", pb[6][:, 0:E],
                               xn_fm[:, ic, qb + tc * 128: qb + (tc + 1) * 128],
                               rwT[:, ic, :], start=(ic == 0), stop=(ic == NIC - 1),
                               inc=("pe" if ic == NIC - 1 else None))
                    P.need("scalar", mm)
                    stok = I("scalar", "activation", out=rp, in_=pb[6][:, 0:E],
                             func=AF.Exp, inc="sc")
                    P.freed(BK(6), stok)
                    P.need("vector", stok)
                    vch("reduce_sum", ssum, rp, axis=AX.X)
                    vch("reciprocal", rsum, ssum)
                    vch("tensor_scalar", out=probs, in0=rp, scalar1=rsum,
                        scalar2=None, op0=ALU.mult)
                    vch("reduce_max", m1, probs, axis=AX.X)
                    vch("tensor_tensor", out=eq1, in0=probs,
                        in1=m1.broadcast_to((128, E)), op=ALU.is_equal)
                    vch("tensor_tensor", out=t1v, in0=eq1,
                        in1=m1.broadcast_to((128, E)), op=ALU.mult)
                    vch("tensor_tensor", out=pwm, in0=probs, in1=t1v, op=ALU.subtract)
                    vch("reduce_max", m2, pwm, axis=AX.X)
                    vch("tensor_tensor", out=keep, in0=pwm,
                        in1=m2.broadcast_to((128, E)), op=ALU.is_equal)
                    vch("tensor_tensor", out=keep, in0=eq1, in1=keep, op=ALU.add)
                    vch("tensor_tensor", out=den, in0=m1, in1=m2, op=ALU.add)
                    vch("tensor_scalar", out=den, in0=den, scalar1=1e-8,
                        scalar2=None, op0=ALU.add)
                    vch("reciprocal", rd, den)
                    vch("tensor_tensor", out=t2v, in0=probs, in1=keep, op=ALU.mult)
                    dtok = vch("tensor_scalar", out=rp, in0=t2v, scalar1=rd,
                               scalar2=None, op0=ALU.mult)
                    P.need("tensor", dtok)
                    P.acquire("tensor", BK(7))
                    ttok = I("tensor", "transpose", pb[7][0:E, 0:128], rp, identf[:],
                             inc="pe")
                    P.need("vector", ttok)
                    vtok = I("vector", "tensor_copy", out=cw_fm[0:E, tc * 128:(tc + 1) * 128],
                             in_=pb[7][0:E, 0:128], inc="dv")
                    P.freed(BK(7), vtok)
                cw_done = ("dv", C("dv"), "vector", False)
                ck(f"router{s}{th}")

                W("tensor", "sc", C("sc"))
                W("tensor", "dv", C("dv"))
                W("tensor", "scw", 16 * ncw[0])
                W("vector", "scw", 16 * ncw[0])
                P.need("tensor", ones_ready)
                CUT = 110.0
                slopes_py = _slopes(H)

                def mk_tiles(gq0, h):
                    tl = []
                    for j in range(gq0 + 1):
                        dmin = 128 * gq0 - (128 * j + 127)
                        if j < gq0 and slopes_py[h] * dmin > CUT:
                            continue
                        tl.append((j, 0, 512, j == gq0))
                    for qt in range(1, 4):
                        for j in range(gq0 + 1, gq0 + qt + 1):
                            dmin = 128 * (gq0 + qt) - (128 * j + 127)
                            if j < gq0 + qt and slopes_py[h] * dmin > CUT:
                                continue
                            tl.append((j, qt * 128, 128, j == gq0 + qt))
                    return tl

                for tqb in range(TH // 512):
                    gq0 = th * 8 + tqb * 4
                    for hp2 in range(H // 2):
                        chains = []
                        for c in range(2):
                            h = 2 * hp2 + c
                            skip_cs = slopes_py[h] * 258.0 <= 45.0
                            csh_c = csh[:] if c == 0 else cshB
                            radj = rshadj if c == 0 else rshadjB
                            if skip_cs:
                                P.acquire("vector", f"rshadj{c}")
                                cst = I("vector", "tensor_scalar", out=radj,
                                        in0=rowshift[:, h * 16:(h + 1) * 16],
                                        scalar1=float(-slopes_py[h] *
                                                      (gq0 * 128 + 255.5)),
                                        scalar2=None, op0=ALU.add, inc="dv")
                            else:
                                P.acquire("vector", f"csh{c}")
                                cst = I("vector", "tensor_scalar", out=csh_c,
                                        in0=tqbf[0:1, gq0 * 128: gq0 * 128 + 512],
                                        scalar1=("NEGSLOPE", h), scalar2=None,
                                        op0=ALU.mult, inc="dv")
                            chains.append(dict(
                                h=h, c=c, skip_cs=skip_cs, csh=csh_c, radj=radj,
                                cst=cst, tiles=mk_tiles(gq0, h),
                                stb=[0, 1] if c == 0 else [6, 7],
                                pvb=2 + c, rbb=4 + c,
                                pslot=[2 * c, 2 * c + 1]))
                        nmax = max(len(ch["tiles"]) for ch in chains)
                        for k in range(nmax):
                            live = [ch for ch in chains if k < len(ch["tiles"])]
                            for ch in live:
                                h, c = ch["h"], ch["c"]
                                hp, hc2 = h % 2, h // 2
                                pbase = 64 * hp
                                j, coff, cwid, diag = ch["tiles"][k]
                                stb = ch["stb"][k % 2]
                                P.need("tensor", ch["cst"])
                                P.acquire("tensor", BK(stb))
                                if not ch["skip_cs"]:
                                    I("tensor", "matmul", pb[stb][:, 0:cwid],
                                      onesb[0:1, 0:128],
                                      (ch["csh"])[0:1, coff:coff + cwid],
                                      start=True, stop=False)
                                mm = I("tensor", "matmul", pb[stb][:, 0:cwid],
                                       k_fm[pbase:pbase + 64, hc2,
                                            j * 128:(j + 1) * 128],
                                       q_fm[pbase:pbase + 64, hc2,
                                            tqb * 512 + coff: tqb * 512 + coff + cwid],
                                       start=ch["skip_cs"], stop=True, inc="pe")
                                ch["mm"] = mm
                            for ch in live:
                                h, c = ch["h"], ch["c"]
                                j, coff, cwid, diag = ch["tiles"][k]
                                stb = ch["stb"][k % 2]
                                exp_src = ch["mm"]
                                if diag:
                                    P.need("vector", ch["mm"])
                                    exp_src = I("vector", "tensor_tensor",
                                                out=pb[stb][:, 0:128],
                                                in0=pb[stb][:, 0:128],
                                                in1=minf[:], op=ALU.add, inc="dv")
                                P.need("scalar", exp_src)
                                slot = ch["pslot"][k % 2]
                                P.acquire("scalar", f"p_t{slot}")
                                bias_ap = (ch["radj"][:, j:j + 1] if ch["skip_cs"]
                                           else rowshift[:, h * 16 + j:
                                                         h * 16 + j + 1])
                                etok = I("scalar", "activation",
                                         out=p_t[:, slot, 0:cwid],
                                         in_=pb[stb][:, 0:cwid], func=AF.Exp,
                                         bias=bias_ap, scale=1.0, inc="sc")
                                P.freed(BK(stb), etok)
                                ch["etok"] = etok
                            for ch in live:
                                h, c = ch["h"], ch["c"]
                                j, coff, cwid, diag = ch["tiles"][k]
                                ntile = len(ch["tiles"])
                                slot = ch["pslot"][k % 2]
                                P.need("tensor", ch["etok"])
                                if k == 0:
                                    P.acquire("tensor", BK(ch["pvb"]))
                                pvtok = I("tensor", "matmul",
                                          pb[ch["pvb"]][0:65, coff:coff + cwid],
                                          v_tm[:, j, 65 * h: 65 * h + 65],
                                          p_t[:, slot, 0:cwid],
                                          start=(k == 0), stop=(k == ntile - 1),
                                          inc="pe")
                                P.freed(f"p_t{slot}", pvtok)
                                ch["pvtok"] = pvtok
                        for ch in chains:
                            h, c = ch["h"], ch["c"]
                            hp, hc2 = h % 2, h // 2
                            pbase = 64 * hp
                            ntile = len(ch["tiles"])
                            pvb, rbb = ch["pvb"], ch["rbb"]
                            P.freed(f"rshadj{c}" if ch["skip_cs"] else f"csh{c}",
                                    ch["pvtok"])
                            P.need("vector", ch["pvtok"])
                            P.acquire("vector", f"rrow{c}")
                            rtok = vch("reciprocal", rrow[0:1, 256 * c:256 * c + 512]
                                       if False else rrow[0:1, :] if c == 0 else
                                       rrowB[0:1, :], pb[pvb][64:65, 0:512])
                            slot = ch["pslot"][ntile % 2]
                            P.acquire("vector", f"p_t{slot}")
                            ctok = vch("tensor_copy", out=p_t[0:64, slot, :],
                                       in_=pb[pvb][0:64, 0:512])
                            P.freed(BK(pvb), ctok)
                            P.need("tensor", rtok)
                            P.acquire("tensor", BK(rbb))
                            rbmm = I("tensor", "matmul", pb[rbb][:],
                                     onesf[0:1, 0:128],
                                     rrow[0:1, :] if c == 0 else rrowB[0:1, :],
                                     start=True, stop=True, inc="pe")
                            P.freed(f"rrow{c}", rbmm)
                            P.need("vector", rbmm)
                            P.acquire("vector", "attn_fmb")
                            ftok = I("vector", "tensor_tensor",
                                     out=attn_fmb[pbase:pbase + 64, hc2, :],
                                     in0=p_t[0:64, slot, :],
                                     in1=pb[rbb][0:64, :], op=ALU.mult, inc="dv")
                            P.freed(BK(rbb), ftok)
                            P.freed(f"p_t{slot}", ftok)
                    P.need("tensor", ("dv", C("dv"), "vector", False))
                    P.need("tensor", cw_done)
                    P.need("tensor", tok_wo)
                    for oc in range(NIC):
                        bank = oc % 2
                        P.acquire("tensor", BK(bank))
                        for ic in range(NIC):
                            I("tensor", "matmul", pb[bank][:],
                              wstgB[:, ic, oc * 128:(oc + 1) * 128],
                              attn_fmb[:, ic, :], start=(ic == 0), stop=False)
                        mm = I("tensor", "matmul", pb[bank][:],
                               b2bf[0:E, oc * 128:(oc + 1) * 128],
                               cw_fm[0:E, tqb * 512:(tqb + 1) * 512], start=False,
                               stop=True, inc="pe")
                        P.need("scalar", mm)
                        P.acquire("scalar", "out_acc_w")
                        stok = I("scalar", "activation",
                                 out=out_acc[:, oc, tqb * 512:(tqb + 1) * 512],
                                 in_=pb[bank][:], func=AF.Identity,
                                 bias=bopt[:, oc:oc + 1], inc="sc")
                        P.freed(BK(bank), stok)
                    P.freed("attn_fmb", ("pe", C("pe"), "tensor", False))
                wo_all_done = ("sc", C("sc"), "scalar", False)
                ck(f"attn{s}{th}")
                attn_pe_done = ("pe", C("pe"), "tensor", False)
                P.freed("wstgB", attn_pe_done)
                P.freed("kreg", attn_pe_done)
                P.freed("sqbuf", attn_pe_done)

                for e in range(E):
                    for half in range(2):
                        P.acquire("sync", "cwrow")
                        P.need("sync", cw_done)
                        P.need("sync", attn_pe_done)
                        W("sync", "scw", 16 * ncw[0])
                        ncw[0] += 1
                        I("sync", "dma_start", out=cwrow,
                          in_=cw_fm[e:e + 1, half * 512:(half + 1) * 512],
                          inc="scw", by=16)
                        cwr_tok = ("scw", 16 * ncw[0], "sync", True)
                        P.need("tensor", cwr_tok)
                        P.acquire("tensor", BK(6))
                        mm = I("tensor", "matmul", pb[6][:], onesb[0:1, 0:128],
                               cwrow[0:1, :], start=True, stop=True, inc="pe")
                        P.freed("cwrow", ("pe", C("pe"), "tensor", False))
                        P.need("vector", mm)
                        vtok = I("vector", "tensor_copy",
                                 out=cwb[:, half * 512:(half + 1) * 512],
                                 in_=pb[6][:], inc="dv")
                        P.freed(BK(6), vtok)
                    cwb_tok = ("dv", C("dv"), "vector", False)

                    for hb in range(NHB):
                        f = e * NHB + hb
                        sl = f % 2
                        if sl == 0:
                            P.acquire("gpsimd", "wstgA")
                            P.acquire("gpsimd", "wstgB")
                        else:
                            P.acquire("gpsimd", "kreg")
                        W("gpsimd", S_SW[sl], 16 * nsw[sl])
                        nsw[sl] += 1
                        I("gpsimd", "dma_start", out=w1S[sl][:, :, :],
                          in_=d_w1T[e, :, hb * HB:(hb + 1) * HB].rearrange(
                              "(c p) n -> p c n", p=128), inc=S_SW[sl], by=16)
                        nsw[sl] += 1
                        I("gpsimd", "dma_start", out=w2S[sl][:, :, :],
                          in_=d_w2T[e, hb * HB:(hb + 1) * HB, :].rearrange(
                              "(c p) n -> p c n", p=128), inc=S_SW[sl], by=16)
                        wtok = (S_SW[sl], 16 * nsw[sl], "gpsimd", True)
                        P.need("tensor", wtok)
                        for tb in range(NTB):
                            gtoks = []
                            for hc in range(HCB):
                                hbank = hc % 2
                                P.acquire("tensor", BK(hbank))
                                for ic in range(NIC):
                                    mm = I("tensor", "matmul", pb[hbank][:],
                                           w1S[sl][:, ic, hc * 128:(hc + 1) * 128],
                                           xn_fm[:, ic, qb + tb * TB: qb + (tb + 1) * TB],
                                           start=(ic == 0), stop=(ic == NIC - 1),
                                           inc=("pe" if ic == NIC - 1 else None))
                                P.need("scalar", mm)
                                if hc == 0:
                                    P.acquire("scalar", "h_all")
                                gtok = I("scalar", "activation", out=h_all[:, hc, :],
                                         in_=pb[hbank][:], func=AF.Gelu_apprx_tanh,
                                         bias=b1pt[:, e, hb * HCB + hc: hb * HCB + hc + 1],
                                         inc="sc")
                                gtoks.append(gtok)
                                P.freed(BK(hbank), gtok)
                            for half in range(2):
                                for hc in range(HCB):
                                    P.need("tensor", gtoks[hc])
                                    for oc4 in range(4):
                                        oc = half * 4 + oc4
                                        obank = 2 + oc4
                                        if hc == 0:
                                            P.acquire("tensor", BK(obank))
                                        mm2 = I("tensor", "matmul", pb[obank][:],
                                                w2S[sl][:, hc, oc * 128:(oc + 1) * 128],
                                                h_all[:, hc, :], start=(hc == 0),
                                                stop=(hc == HCB - 1),
                                                inc=("pe" if (hc == HCB - 1 and oc4 == 3)
                                                     else None))
                                P.need("vector", mm2)
                                W("vector", "dv", cwb_tok[1])
                                P.need("vector", wo_all_done)
                                for oc4 in range(4):
                                    vtok = I("vector", "tensor_tensor",
                                             out=pb[2 + oc4][:], in0=pb[2 + oc4][:],
                                             in1=cwb[:, tb * TB:(tb + 1) * TB],
                                             op=ALU.mult, inc="dv")
                                W("vector", "dv", vtok[1])
                                for oc4 in range(4):
                                    oc = half * 4 + oc4
                                    atok = I("vector", "tensor_tensor",
                                      out=out_acc[:, oc, tb * TB:(tb + 1) * TB],
                                      in0=out_acc[:, oc, tb * TB:(tb + 1) * TB],
                                      in1=pb[2 + oc4][:], op=ALU.add,
                                      inc="dv")
                                    P.freed(BK(2 + oc4), atok)
                            P.freed("h_all", ("pe", C("pe"), "tensor", False))
                        consumed = ("pe", C("pe"), "tensor", False)
                        if sl == 0:
                            P.freed("wstgA", consumed)
                            P.freed("wstgB", consumed)
                        else:
                            P.freed("kreg", consumed)
                ffn_done = ("dv", C("dv"), "vector", False)
                ck(f"ffn{s}{th}")

                for tc in range(TH // 128):
                    p = tc % 2
                    P.acquire("sync", f"x_stage{p}")
                    P.need("sync", attn_pe_done)
                    W("sync", S_SX[p], 16 * nx[p])
                    nx[p] += 1
                    I("sync", "dma_start", out=x_stage[:, p, :],
                      in_=d_x[s, qb + tc * 128: qb + (tc + 1) * 128, :], inc=S_SX[p], by=16)
                    xtok = (S_SX[p], 16 * nx[p], "sync", True)
                    P.need("tensor", ffn_done)
                    vtok = None
                    for oc in range(NIC):
                        bank = 6 + oc % 2
                        P.acquire("tensor", BK(bank))
                        ttok = I("tensor", "transpose", pb[bank][:, 0:128],
                                 out_acc[:, oc, tc * 128:(tc + 1) * 128], identf[:],
                                 inc="pe")
                        P.need("vector", ttok)
                        P.need("vector", xtok)
                        if oc == 0:
                            P.acquire("vector", f"out_tm{p}")
                        vtok = I("vector", "tensor_tensor",
                                 out=out_tm[:, p, oc * 128:(oc + 1) * 128],
                                 in0=pb[bank][:, 0:128],
                                 in1=x_stage[:, p, oc * 128:(oc + 1) * 128],
                                 op=ALU.add, inc="dv")
                        P.freed(BK(bank), vtok)
                        if oc == NIC - 1:
                            P.freed(f"x_stage{p}", vtok)
                    P.need("sync", vtok)
                    W("sync", S_ST[p], 16 * nst[p])
                    nst[p] += 1
                    I("sync", "dma_start",
                      out=d_out[s, qb + tc * 128: qb + (tc + 1) * 128, :],
                      in_=out_tm[:, p, :], inc=S_ST[p], by=16)
                    P.freed(f"out_tm{p}", (S_ST[p], 16 * nst[p], "sync", True))
                P.freed("out_acc_w", ("pe", C("pe"), "tensor", False))
                ck(f"final{s}{th}")

        except _StopPlan:
            pass

        if upto is not None:
            for sname in ["pe", "sc", "dv", "sg"]:
                W("sync", sname, C(sname))
            dumps = [("dbg_xn", xn_fm[:, :, :], [128, NIC, S], BF16),
                     ("dbg_q", q_fm, [128, NIC, TH], BF16),
                     ("dbg_k", k_fm, [128, NIC, S], BF16),
                     ("dbg_v", v_tm, [128, S // 128, H * 65], BF16),
                     ("dbg_cw", cw_fm[:, :], [E, TH], BF16),
                     ("dbg_acc", out_acc[:, :, :], [128, NIC, TH], F32)]
            for nm, src, shp, dt in dumps:
                dd = nc.dram_tensor(nm, shp, dt, kind="ExternalOutput").ap()
                nst[0] += 1
                I("sync", "dma_start", out=dd, in_=src, inc=S_ST[0], by=16)

        W("sync", S_ST[0], 16 * nst[0])
        W("sync", S_ST[1], 16 * nst[1])

        for e_ in ENGS:
            for op in P.ops[e_]:
                if op[0] == "w":
                    assert op[2] <= C(op[1]), (e_, op, C(op[1]))

        slopes = _slopes(H)

        def replay(eng_name, eng):
            for op in P.ops[eng_name]:
                if op[0] == "w":
                    eng.wait_ge(sems[op[1]], op[2])
                else:
                    _, method, args, kw, inc, by = op
                    kw2 = {}
                    for k, v in kw.items():
                        if isinstance(v, tuple) and len(v) == 2 and v[0] == "NEGSLOPE":
                            v = float(-slopes[v[1]])
                        kw2[k] = v
                    inst = getattr(eng, method)(*args, **kw2)
                    if inc:
                        inst.then_inc(sems[inc], by)

        block = es.enter_context(nc.Block())

        @block.sync
        def _(eng):
            replay("sync", eng)

        @block.gpsimd
        def _(eng):
            replay("gpsimd", eng)

        @block.tensor
        def _(eng):
            replay("tensor", eng)

        @block.scalar
        def _(eng):
            replay("scalar", eng)

        @block.vector
        def _(eng):
            replay("vector", eng)

    return nc


def _prep_inputs(ins):
    f = np.float32
    slopes = _slopes(H)
    wqT = np.ascontiguousarray(ins["L0_attn_Wq"].T)
    wkT = np.ascontiguousarray(ins["L0_attn_Wk"].T)
    wvT = np.ascontiguousarray(ins["L0_attn_Wv"].T)
    woT = np.ascontiguousarray(ins["L0_attn_Wo"].T)
    w1T = np.ascontiguousarray(ins["L0_ffn_w1"].transpose(0, 2, 1))
    w2T = np.ascontiguousarray(ins["L0_ffn_w2"].transpose(0, 2, 1))
    rwT = np.ascontiguousarray(ins["L0_ffn_router_w"].T)
    bqpt = np.ascontiguousarray(ins["L0_attn_bq"].reshape(NIC, 128).T)
    bkpt = np.ascontiguousarray(ins["L0_attn_bk"].reshape(NIC, 128).T)
    bopt = np.ascontiguousarray(ins["L0_attn_bo"].reshape(NIC, 128).T)
    bvr = np.ascontiguousarray(ins["L0_attn_bv"].reshape(1, D))
    b1pt = np.ascontiguousarray(
        ins["L0_ffn_b1"].reshape(E, HID // 128, 128).transpose(2, 0, 1))
    b2 = np.ascontiguousarray(ins["L0_ffn_b2"])
    identb = np.eye(128).astype(ml_dtypes.bfloat16)
    identf = np.eye(128, dtype=f)
    jj, pp = np.meshgrid(np.arange(S // 128, dtype=f), np.arange(128, dtype=f))
    rowshift = np.zeros((128, H * (S // 128)), f)
    for h in range(H):
        rowshift[:, h * 16:(h + 1) * 16] = slopes[h] * (128.0 * jj + pp)
    minf = np.tril(np.full((128, 128), -1e30, f), -1)
    tqrow = np.arange(S, dtype=f).reshape(1, S)
    shared = dict(wqT=wqT, wkT=wkT, wvT=wvT, woT=woT, w1T=w1T, w2T=w2T, rwT=rwT,
                  bqpt=bqpt, bkpt=bkpt, bopt=bopt, bvr=bvr, b1pt=b1pt, b2=b2,
                  identb=identb, identf=identf, rowshift=rowshift, minf=minf,
                  tqrow=tqrow)
    x = np.ascontiguousarray(ins["L0_attn_x"])
    in_maps = []
    for c in range(NCORE):
        m = dict(shared)
        m["x"] = np.ascontiguousarray(x[c * BPC:(c + 1) * BPC])
        in_maps.append(m)
    return in_maps


def kernel(**ins):
    nc = build_core_program()
    in_maps = _prep_inputs(ins)
    res = run_bass_kernel_spmd(nc, in_maps, core_ids=list(range(NCORE)))
    return np.concatenate([r["out"] for r in res.results], axis=0)


# version 16
# speedup vs baseline: 1.0078x; 1.0078x over previous
import math
from contextlib import ExitStack

import numpy as np
import ml_dtypes

import concourse.bass as bass
import concourse.mybir as mybir
from concourse.bass_utils import run_bass_kernel_spmd

F32 = mybir.dt.float32
BF16 = mybir.dt.bfloat16
AF = mybir.ActivationFunctionType
ALU = mybir.AluOpType
AX = mybir.AxisListType

B, S, D = 16, 2048, 1024
NCORE = 8
BPC = B // NCORE
H, HD = 16, 64
E, HID = 8, 4096
NIC = D // 128
NTH = 2
TH = S // NTH
HB = 1024
NHB = HID // HB
HCB = HB // 128
TB = 512
NTB = TH // TB

ENGS = ["tensor", "scalar", "vector", "gpsimd", "sync"]


def _slopes(n):
    start = 2.0 ** (-(2.0 ** (-(math.log2(n) - 3))))
    return np.array([start * start ** i for i in range(n)], dtype=np.float32)


class Plan:

    def __init__(self):
        self.ops = {e: [] for e in ENGS}
        self.cnt = {}
        self.emitted = {e: {} for e in ENGS}
        self.gate = {}

    def C(self, sem):
        return self.cnt.get(sem, 0)

    def I(self, eng, method, *args, inc=None, by=1, **kw):
        self.ops[eng].append(("i", method, args, kw, inc, by))
        if inc:
            self.cnt[inc] = self.cnt.get(inc, 0) + by
            return (inc, self.cnt[inc], eng, by >= 16)
        return None

    def W(self, eng, sem, val):
        if val <= 0:
            return
        if self.emitted[eng].get(sem, -1) >= val:
            return
        self.emitted[eng][sem] = val
        self.ops[eng].append(("w", sem, val))

    def need(self, eng, tok):
        if tok is None:
            return
        sem, val, src_eng, is_dma = tok
        if (not is_dma) and src_eng == eng:
            return
        self.W(eng, sem, val)

    def freed(self, key, tok):
        self.gate[key] = tok

    def acquire(self, eng, key):
        self.need(eng, self.gate.get(key))


class _StopPlan(Exception):
    pass


def build_core_program(upto=None):
    nc = bass.Bass("TRN2", target_bir_lowering=False)

    d_x = nc.dram_tensor("x", [BPC, S, D], F32, kind="ExternalInput").ap()
    d_out = nc.dram_tensor("out", [BPC, S, D], F32, kind="ExternalOutput").ap()
    d_wqT = nc.dram_tensor("wqT", [D, D], F32, kind="ExternalInput").ap()
    d_wkT = nc.dram_tensor("wkT", [D, D], F32, kind="ExternalInput").ap()
    d_wvT = nc.dram_tensor("wvT", [D, D], F32, kind="ExternalInput").ap()
    d_woT = nc.dram_tensor("woT", [D, D], F32, kind="ExternalInput").ap()
    d_w1T = nc.dram_tensor("w1T", [E, D, HID], F32, kind="ExternalInput").ap()
    d_w2T = nc.dram_tensor("w2T", [E, HID, D], F32, kind="ExternalInput").ap()
    d_rwT = nc.dram_tensor("rwT", [D, E], F32, kind="ExternalInput").ap()
    d_bqpt = nc.dram_tensor("bqpt", [128, NIC], F32, kind="ExternalInput").ap()
    d_bkpt = nc.dram_tensor("bkpt", [128, NIC], F32, kind="ExternalInput").ap()
    d_bopt = nc.dram_tensor("bopt", [128, NIC], F32, kind="ExternalInput").ap()
    d_bv = nc.dram_tensor("bvr", [1, D], F32, kind="ExternalInput").ap()
    d_b1pt = nc.dram_tensor("b1pt", [128, E, HID // 128], F32, kind="ExternalInput").ap()
    d_b2 = nc.dram_tensor("b2", [E, D], F32, kind="ExternalInput").ap()
    d_identb = nc.dram_tensor("identb", [128, 128], BF16, kind="ExternalInput").ap()
    d_identf = nc.dram_tensor("identf", [128, 128], F32, kind="ExternalInput").ap()
    d_rowshift = nc.dram_tensor("rowshift", [128, H * (S // 128)], F32, kind="ExternalInput").ap()
    d_minf = nc.dram_tensor("minf", [128, 128], F32, kind="ExternalInput").ap()
    d_tq = nc.dram_tensor("tqrow", [1, S], F32, kind="ExternalInput").ap()

    es = ExitStack()
    sb = lambda name, shape, dt: es.enter_context(nc.sbuf_tensor("s_" + name, shape, dt))
    psal = lambda name, shape, dt: es.enter_context(nc.psum_tensor(name, shape, dt))
    mksem = lambda name: es.enter_context(nc.semaphore(name))

    with es:
        identb = sb("identb", [128, 128], BF16)
        identf = sb("identf", [128, 128], F32)
        rowshift = sb("rowshift", [128, H * (S // 128)], F32)
        minf = sb("minf", [128, 128], F32)
        bqpt = sb("bqpt", [128, NIC], F32)
        bkpt = sb("bkpt", [128, NIC], F32)
        bopt = sb("bopt", [128, NIC], F32)
        bq8 = sb("bq8", [128, NIC], F32)
        b1pt = sb("b1pt", [128, E, HID // 128], F32)
        b2bf = sb("b2bf", [E, D], BF16)
        rwT = sb("rwT", [128, NIC, E], BF16)
        bvrow = sb("bvrow", [1, D], BF16)
        tqbf = sb("tqbf", [1, S], BF16)
        onesb = sb("onesb", [1, 128], BF16)
        onesf = sb("onesf", [1, 128], F32)
        csh = sb("csh", [1, 512], BF16)
        epsv = sb("epsv", [128, 1], F32)
        xn_fm = sb("xn_fm", [128, NIC, S], BF16)
        out_acc = sb("out_acc", [128, NIC, TH], F32)
        cw_fm = sb("cw_fm", [E, TH], BF16)

        AR_N = 63744
        arena = sb("arena", [128, AR_N], BF16)
        arb = arena.ap()
        arf = arena.ap().bitcast(F32)

        def bfv(off, shape):
            n = int(np.prod(shape))
            apx = arb[:, off:off + n]
            if len(shape) == 1:
                return apx
            if len(shape) == 2:
                return apx.rearrange("p (a b) -> p a b", b=shape[1])
            return apx.rearrange("p (a b c) -> p a b c", b=shape[1], c=shape[2])

        def f32v(off_bf, shape):
            off = off_bf // 2
            n = int(np.prod(shape))
            apx = arf[:, off:off + n]
            if len(shape) == 1:
                return apx
            return apx.rearrange("p (a b) -> p a b", b=shape[1])

        q_fm = bfv(0, (NIC, TH))
        k_fm = bfv(8192, (NIC, S))
        v_tm = bfv(24576, (S // 128, H * 65))
        wstgA = bfv(41216, (NIC, D))
        wstgB = bfv(49408, (NIC, D))
        attn_fmb = bfv(57600, (NIC, 512))
        p_t = bfv(61696, (4, 512))
        cwrow = arb[0:1, 32768:33280]
        w1A = bfv(41216, (NIC, HB))
        w2A = bfv(49408, (HCB, D))
        w1B = bfv(8192, (NIC, HB))
        w2B = bfv(16384, (HCB, D))
        w1S, w2S = [w1A, w1B], [w2A, w2B]
        h_all = bfv(24576, (HCB, 512))
        cwb = bfv(36864, (D,))
        sqbuf = f32v(45312, (D,))
        rrow = f32v(43264, (512,))
        rrowB = f32v(46336, (512,))
        out_tm = f32v(0, (2, D))
        x_stage = f32v(4096, (2, D))
        xn_tm = bfv(41216, (2, D))
        rp = f32v(44288, (E,))
        probs = f32v(44304, (E,))
        eq1 = f32v(44320, (E,))
        t1v = f32v(44336, (E,))
        pwm = f32v(44352, (E,))
        t2v = f32v(44368, (E,))
        keep = f32v(44384, (E,))
        m1 = f32v(44400, (1,))
        m2 = f32v(44408, (1,))
        den = f32v(44416, (1,))
        rd = f32v(44424, (1,))
        ssum = f32v(44432, (1,))
        rsum = f32v(44440, (1,))
        bn6 = f32v(44448, (12,))
        m2v = f32v(44480, (1,))
        varv = f32v(44488, (1,))
        mv = f32v(44496, (2,))
        rshadj = f32v(44528, (16,))
        rshadjB = f32v(44576, (16,))
        cshB = arb[0:1, 44640:45152]
        sqv = f32v(44504, (1,))
        rstd = f32v(44512, (1,))
        mrs = f32v(44520, (1,))

        pb = [psal(f"pb{i}", [128, 512], F32) for i in range(8)]
        pbb = [p.ap().bitcast(BF16) for p in pb]

        SEM_NAMES = ["ld", "ldg", "sg", "sx0", "sx1", "sw0", "sw1", "st0", "st1",
                     "scw", "pe", "sc", "dv"]
        sems = {n: mksem(n) for n in SEM_NAMES}
        S_SX, S_SW, S_ST = ["sx0", "sx1"], ["sw0", "sw1"], ["st0", "st1"]

        P = Plan()
        I, W, C = P.I, P.W, P.C
        BK = lambda i: f"bank{i}"

        def ck(name):
            if upto is not None and upto == name:
                raise _StopPlan()

        def vch(method, *args, **kw):
            tk = I("vector", method, *args, inc="dv", **kw)
            W("vector", "dv", tk[1])
            return tk

        for dst, src in [(identb, d_identb), (identf, d_identf), (rowshift, d_rowshift),
                         (minf, d_minf), (bqpt, d_bqpt), (bkpt, d_bkpt), (bopt, d_bopt),
                         (b1pt, d_b1pt)]:
            I("sync", "dma_start", out=dst[:], in_=src[:], inc="ld", by=16)
        n_const = C("ld")
        for dst, src in [(b2bf, d_b2), (bvrow, d_bv), (tqbf, d_tq)]:
            I("gpsimd", "dma_start", out=dst[:], in_=src[:], inc="ldg", by=16)
        I("gpsimd", "dma_start", out=rwT[:], in_=d_rwT.rearrange("(c p) e -> p c e", p=128),
          inc="ldg", by=16)
        n_constg = C("ldg")
        I("gpsimd", "memset", onesb[:], 1.0)
        I("gpsimd", "memset", onesf[:], 1.0, inc="sg")
        n_sg0 = C("sg")
        for e_ in ["tensor", "scalar", "vector", "gpsimd"]:
            W(e_, "ld", n_const)
            W(e_, "ldg", n_constg)
            W(e_, "sg", n_sg0)
        I("vector", "memset", epsv[:], 1e-5)
        tok_bq8 = I("vector", "tensor_scalar", out=bq8[:], in0=bqpt[:], scalar1=0.125,
                    scalar2=None, op0=ALU.mult, inc="dv")

        nx = [0, 0]
        nst = [0, 0]
        ncw = [0]
        nsw = [0, 0]

        try:
          for s in range(BPC):
            W("vector", "pe", C("pe"))
            for tc in range(S // 128):
                p = tc % 2
                P.acquire("sync", f"x_stage{p}")
                W("sync", S_SX[p], 16 * nx[p])
                nx[p] += 1
                I("sync", "dma_start", out=x_stage[:, p, :],
                  in_=d_x[s, tc * 128:(tc + 1) * 128, :], inc=S_SX[p], by=16)
                xtok = (S_SX[p], 16 * nx[p], "sync", True)
                P.need("vector", xtok)
                P.need("scalar", xtok)
                tok = vch("reduce_sum", mv[:, 0:1], x_stage[:, p, :], axis=AX.X)
                P.acquire("scalar", "sqbuf")
                stok = I("scalar", "activation", out=sqbuf, in_=x_stage[:, p, :],
                         func=AF.Square, inc="sc")
                P.need("vector", stok)
                tok2 = vch("reduce_sum", mv[:, 1:2], sqbuf, axis=AX.X)
                P.freed("sqbuf", tok2)
                vch("tensor_scalar", out=mv, in0=mv, scalar1=1.0 / D,
                    scalar2=None, op0=ALU.mult)
                vch("tensor_tensor", out=m2v, in0=mv[:, 0:1], in1=mv[:, 0:1],
                    op=ALU.mult)
                tok = vch("tensor_tensor", out=varv, in0=mv[:, 1:2], in1=m2v,
                          op=ALU.subtract)
                P.need("scalar", tok)
                tok = I("scalar", "activation", out=sqv, in_=varv, func=AF.Sqrt,
                        bias=epsv[:], scale=1.0, inc="sc")
                P.need("vector", tok)
                vch("reciprocal", rstd, sqv)
                vch("tensor_tensor", out=mrs, in0=mv[:, 0:1], in1=rstd, op=ALU.mult)
                P.acquire("vector", f"xn_tm{p}")
                tok_xn = I("vector", "tensor_scalar", out=xn_tm[:, p, :], in0=x_stage[:, p, :],
                           scalar1=rstd, scalar2=mrs, op0=ALU.mult, op1=ALU.subtract,
                           inc="dv")
                P.freed(f"x_stage{p}", tok_xn)
                P.need("tensor", tok_xn)
                for oc in range(NIC):
                    bank = 6 + oc % 2
                    P.acquire("tensor", BK(bank))
                    ttok = I("tensor", "transpose", pbb[bank][:, 0:128],
                             xn_tm[:, p, oc * 128:(oc + 1) * 128], identb[:], inc="pe")
                    P.need("vector", ttok)
                    vtok = I("vector", "tensor_copy", out=xn_fm[:, oc, tc * 128:(tc + 1) * 128],
                             in_=pbb[bank][:, 0:128], inc="dv")
                    P.freed(BK(bank), vtok)
                    if oc == NIC - 1:
                        P.freed(f"xn_tm{p}", ttok)
            xn_done = ("dv", C("dv"), "vector", False)
            P.freed("wstgA", ("pe", C("pe"), "tensor", False))
            ck(f"ln{s}")

            for th in range(NTH):
                qb = th * TH
                kv = (th + 1) * TH
                nkc = kv // 128

                def stg_fill(slot_i, wsrc):
                    P.acquire("gpsimd", "wstgA" if slot_i == 0 else "wstgB")
                    W("gpsimd", S_SW[slot_i], 16 * nsw[slot_i])
                    nsw[slot_i] += 1
                    I("gpsimd", "dma_start", out=(wstgA if slot_i == 0 else wstgB)[:, :, :],
                      in_=wsrc.rearrange("(c p) n -> p c n", p=128), inc=S_SW[slot_i], by=16)
                    return (S_SW[slot_i], 16 * nsw[slot_i], "gpsimd", True)

                tok_wq = stg_fill(0, d_wqT)
                tok_wk = stg_fill(1, d_wkT)

                W("scalar", S_SX[0], 16 * nx[0])
                W("scalar", S_SX[1], 16 * nx[1])
                W("scalar", S_ST[0], 16 * nst[0])
                W("scalar", S_ST[1], 16 * nst[1])
                W("scalar", "dv", C("dv"))

                P.need("tensor", xn_done)
                P.need("tensor", tok_wq)
                P.need("scalar", tok_bq8)
                for u, (tqb, oc) in enumerate([(a, b) for a in range(TH // 512)
                                               for b in range(NIC)]):
                    bank = u % 4
                    P.acquire("tensor", BK(bank))
                    for ic in range(NIC):
                        mm = I("tensor", "matmul", pb[bank][:],
                               wstgA[:, ic, oc * 128:(oc + 1) * 128],
                               xn_fm[:, ic, qb + tqb * 512: qb + (tqb + 1) * 512],
                               start=(ic == 0), stop=(ic == NIC - 1),
                               inc=("pe" if ic == NIC - 1 else None))
                    P.need("scalar", mm)
                    stok = I("scalar", "activation", out=q_fm[:, oc, tqb * 512:(tqb + 1) * 512],
                             in_=pb[bank][:], func=AF.Identity, bias=bq8[:, oc:oc + 1],
                             scale=0.125, inc="sc")
                    P.freed(BK(bank), stok)
                P.freed("wstgA", ("pe", C("pe"), "tensor", False))

                P.need("tensor", tok_wk)
                P.acquire("scalar", "kreg")
                for u, (kb, oc) in enumerate([(a, b) for a in range(kv // 512)
                                              for b in range(NIC)]):
                    bank = u % 4
                    P.acquire("tensor", BK(bank))
                    for ic in range(NIC):
                        mm = I("tensor", "matmul", pb[bank][:],
                               wstgB[:, ic, oc * 128:(oc + 1) * 128],
                               xn_fm[:, ic, kb * 512:(kb + 1) * 512],
                               start=(ic == 0), stop=(ic == NIC - 1),
                               inc=("pe" if ic == NIC - 1 else None))
                    P.need("scalar", mm)
                    stok = I("scalar", "activation", out=k_fm[:, oc, kb * 512:(kb + 1) * 512],
                             in_=pb[bank][:], func=AF.Identity, bias=bkpt[:, oc:oc + 1],
                             scale=1.0, inc="sc")
                    P.freed(BK(bank), stok)
                P.freed("wstgB", ("pe", C("pe"), "tensor", False))
                tok_wv = stg_fill(0, d_wvT)
                tok_wo = stg_fill(1, d_woT)

                W("gpsimd", "pe", C("pe"))
                W("gpsimd", "dv", C("dv"))
                W("gpsimd", "sc", C("sc"))
                W("gpsimd", "scw", 16 * ncw[0])
                for tkc in range(nkc):
                    ocols = v_tm[:, tkc, :].rearrange("p (h d) -> p h d", d=65)[:, :, 64:65]
                    I("gpsimd", "memset", ocols, 1.0,
                      inc=("sg" if tkc == nkc - 1 else None))
                ones_ready = ("sg", C("sg"), "gpsimd", False)

                P.need("tensor", tok_wv)
                for u, (tkc, oh) in enumerate([(a, b) for a in range(nkc) for b in range(2)]):
                    bank = u % 4
                    P.acquire("tensor", BK(bank))
                    for ic in range(NIC):
                        I("tensor", "matmul", pb[bank][:],
                          xn_fm[:, ic, tkc * 128:(tkc + 1) * 128],
                          wstgA[:, ic, oh * 512:(oh + 1) * 512],
                          start=(ic == 0), stop=False)
                    mm = I("tensor", "matmul", pb[bank][:], onesb[0:1, 0:128],
                           bvrow[0:1, oh * 512:(oh + 1) * 512], start=False, stop=True,
                           inc="pe")
                    P.need("vector", mm)
                    P.need("vector", ones_ready)
                    dst = v_tm[:, tkc, oh * 520:(oh + 1) * 520].rearrange(
                        "p (h d) -> p h d", d=65)[:, :, 0:64]
                    src = pb[bank].ap().rearrange("p (h d) -> p h d", d=64)
                    vtok = I("vector", "tensor_copy", out=dst, in_=src, inc="dv")
                    P.freed(BK(bank), vtok)
                P.freed("wstgA", ("pe", C("pe"), "tensor", False))

                for tc in range(TH // 128):
                    P.acquire("tensor", BK(6))
                    for ic in range(NIC):
                        mm = I("tensor", "matmul", pb[6][:, 0:E],
                               xn_fm[:, ic, qb + tc * 128: qb + (tc + 1) * 128],
                               rwT[:, ic, :], start=(ic == 0), stop=(ic == NIC - 1),
                               inc=("pe" if ic == NIC - 1 else None))
                    P.need("scalar", mm)
                    stok = I("scalar", "activation", out=rp, in_=pb[6][:, 0:E],
                             func=AF.Exp, inc="sc")
                    P.freed(BK(6), stok)
                    P.need("vector", stok)
                    vch("reduce_sum", ssum, rp, axis=AX.X)
                    vch("reciprocal", rsum, ssum)
                    vch("tensor_scalar", out=probs, in0=rp, scalar1=rsum,
                        scalar2=None, op0=ALU.mult)
                    vch("reduce_max", m1, probs, axis=AX.X)
                    vch("tensor_tensor", out=eq1, in0=probs,
                        in1=m1.broadcast_to((128, E)), op=ALU.is_equal)
                    vch("tensor_tensor", out=t1v, in0=eq1,
                        in1=m1.broadcast_to((128, E)), op=ALU.mult)
                    vch("tensor_tensor", out=pwm, in0=probs, in1=t1v, op=ALU.subtract)
                    vch("reduce_max", m2, pwm, axis=AX.X)
                    vch("tensor_tensor", out=keep, in0=pwm,
                        in1=m2.broadcast_to((128, E)), op=ALU.is_equal)
                    vch("tensor_tensor", out=keep, in0=eq1, in1=keep, op=ALU.add)
                    vch("tensor_tensor", out=den, in0=m1, in1=m2, op=ALU.add)
                    vch("tensor_scalar", out=den, in0=den, scalar1=1e-8,
                        scalar2=None, op0=ALU.add)
                    vch("reciprocal", rd, den)
                    vch("tensor_tensor", out=t2v, in0=probs, in1=keep, op=ALU.mult)
                    dtok = vch("tensor_scalar", out=rp, in0=t2v, scalar1=rd,
                               scalar2=None, op0=ALU.mult)
                    P.need("tensor", dtok)
                    P.acquire("tensor", BK(7))
                    ttok = I("tensor", "transpose", pb[7][0:E, 0:128], rp, identf[:],
                             inc="pe")
                    P.need("vector", ttok)
                    vtok = I("vector", "tensor_copy", out=cw_fm[0:E, tc * 128:(tc + 1) * 128],
                             in_=pb[7][0:E, 0:128], inc="dv")
                    P.freed(BK(7), vtok)
                cw_done = ("dv", C("dv"), "vector", False)
                ck(f"router{s}{th}")

                W("tensor", "sc", C("sc"))
                W("tensor", "dv", C("dv"))
                W("tensor", "scw", 16 * ncw[0])
                W("vector", "scw", 16 * ncw[0])
                P.need("tensor", ones_ready)
                CUT = 110.0
                slopes_py = _slopes(H)

                def mk_tiles(gq0, h):
                    tl = []
                    for j in range(gq0 + 1):
                        dmin = 128 * gq0 - (128 * j + 127)
                        if j < gq0 and slopes_py[h] * dmin > CUT:
                            continue
                        tl.append((j, 0, 512, j == gq0))
                    for qt in range(1, 4):
                        for j in range(gq0 + 1, gq0 + qt + 1):
                            dmin = 128 * (gq0 + qt) - (128 * j + 127)
                            if j < gq0 + qt and slopes_py[h] * dmin > CUT:
                                continue
                            tl.append((j, qt * 128, 128, j == gq0 + qt))
                    return tl

                for tqb in range(TH // 512):
                    gq0 = th * 8 + tqb * 4
                    for hp2 in range(H // 2):
                        chains = []
                        for c in range(2):
                            h = 2 * hp2 + c
                            skip_cs = slopes_py[h] * 258.0 <= 45.0
                            csh_c = csh[:] if c == 0 else cshB
                            radj = rshadj if c == 0 else rshadjB
                            if skip_cs:
                                P.acquire("vector", f"rshadj{c}")
                                cst = I("vector", "tensor_scalar", out=radj,
                                        in0=rowshift[:, h * 16:(h + 1) * 16],
                                        scalar1=float(-slopes_py[h] *
                                                      (gq0 * 128 + 255.5)),
                                        scalar2=None, op0=ALU.add, inc="dv")
                            else:
                                P.acquire("vector", f"csh{c}")
                                cst = I("vector", "tensor_scalar", out=csh_c,
                                        in0=tqbf[0:1, gq0 * 128: gq0 * 128 + 512],
                                        scalar1=("NEGSLOPE", h), scalar2=None,
                                        op0=ALU.mult, inc="dv")
                            chains.append(dict(
                                h=h, c=c, skip_cs=skip_cs, csh=csh_c, radj=radj,
                                cst=cst, tiles=mk_tiles(gq0, h),
                                stb=[0, 1] if c == 0 else [6, 7],
                                pvb=2 + c, rbb=4 + c,
                                pslot=[2 * c, 2 * c + 1]))
                        nmax = max(len(ch["tiles"]) for ch in chains)
                        for k in range(nmax):
                            live = [ch for ch in chains if k < len(ch["tiles"])]
                            for ch in live:
                                stb = ch["stb"][k % 2]
                                P.need("tensor", ch["cst"])
                                P.acquire("tensor", BK(stb))
                            for ch in live:
                                h, c = ch["h"], ch["c"]
                                hp, hc2 = h % 2, h // 2
                                pbase = 64 * hp
                                j, coff, cwid, diag = ch["tiles"][k]
                                stb = ch["stb"][k % 2]
                                if not ch["skip_cs"]:
                                    I("tensor", "matmul", pb[stb][:, 0:cwid],
                                      onesb[0:1, 0:128],
                                      (ch["csh"])[0:1, coff:coff + cwid],
                                      start=True, stop=False)
                                mm = I("tensor", "matmul", pb[stb][:, 0:cwid],
                                       k_fm[pbase:pbase + 64, hc2,
                                            j * 128:(j + 1) * 128],
                                       q_fm[pbase:pbase + 64, hc2,
                                            tqb * 512 + coff: tqb * 512 + coff + cwid],
                                       start=ch["skip_cs"], stop=True, inc="pe")
                                ch["mm"] = mm
                            for ch in live:
                                h, c = ch["h"], ch["c"]
                                j, coff, cwid, diag = ch["tiles"][k]
                                stb = ch["stb"][k % 2]
                                exp_src = ch["mm"]
                                if diag:
                                    P.need("vector", ch["mm"])
                                    exp_src = I("vector", "tensor_tensor",
                                                out=pb[stb][:, 0:128],
                                                in0=pb[stb][:, 0:128],
                                                in1=minf[:], op=ALU.add, inc="dv")
                                P.need("scalar", exp_src)
                                slot = ch["pslot"][k % 2]
                                P.acquire("scalar", f"p_t{slot}")
                                bias_ap = (ch["radj"][:, j:j + 1] if ch["skip_cs"]
                                           else rowshift[:, h * 16 + j:
                                                         h * 16 + j + 1])
                                etok = I("scalar", "activation",
                                         out=p_t[:, slot, 0:cwid],
                                         in_=pb[stb][:, 0:cwid], func=AF.Exp,
                                         bias=bias_ap, scale=1.0, inc="sc")
                                P.freed(BK(stb), etok)
                                ch["etok"] = etok
                            for ch in live:
                                h, c = ch["h"], ch["c"]
                                j, coff, cwid, diag = ch["tiles"][k]
                                ntile = len(ch["tiles"])
                                slot = ch["pslot"][k % 2]
                                P.need("tensor", ch["etok"])
                                if k == 0:
                                    P.acquire("tensor", BK(ch["pvb"]))
                                pvtok = I("tensor", "matmul",
                                          pb[ch["pvb"]][0:65, coff:coff + cwid],
                                          v_tm[:, j, 65 * h: 65 * h + 65],
                                          p_t[:, slot, 0:cwid],
                                          start=(k == 0), stop=(k == ntile - 1),
                                          inc="pe")
                                P.freed(f"p_t{slot}", pvtok)
                                ch["pvtok"] = pvtok
                        for ch in chains:
                            h, c = ch["h"], ch["c"]
                            hp, hc2 = h % 2, h // 2
                            pbase = 64 * hp
                            ntile = len(ch["tiles"])
                            pvb, rbb = ch["pvb"], ch["rbb"]
                            P.freed(f"rshadj{c}" if ch["skip_cs"] else f"csh{c}",
                                    ch["pvtok"])
                            P.need("vector", ch["pvtok"])
                            P.acquire("vector", f"rrow{c}")
                            rtok = vch("reciprocal", rrow[0:1, 256 * c:256 * c + 512]
                                       if False else rrow[0:1, :] if c == 0 else
                                       rrowB[0:1, :], pb[pvb][64:65, 0:512])
                            slot = ch["pslot"][ntile % 2]
                            P.acquire("vector", f"p_t{slot}")
                            ctok = vch("tensor_copy", out=p_t[0:64, slot, :],
                                       in_=pb[pvb][0:64, 0:512])
                            P.freed(BK(pvb), ctok)
                            P.need("tensor", rtok)
                            P.acquire("tensor", BK(rbb))
                            rbmm = I("tensor", "matmul", pb[rbb][:],
                                     onesf[0:1, 0:128],
                                     rrow[0:1, :] if c == 0 else rrowB[0:1, :],
                                     start=True, stop=True, inc="pe")
                            P.freed(f"rrow{c}", rbmm)
                            P.need("vector", rbmm)
                            P.acquire("vector", "attn_fmb")
                            ftok = I("vector", "tensor_tensor",
                                     out=attn_fmb[pbase:pbase + 64, hc2, :],
                                     in0=p_t[0:64, slot, :],
                                     in1=pb[rbb][0:64, :], op=ALU.mult, inc="dv")
                            P.freed(BK(rbb), ftok)
                            P.freed(f"p_t{slot}", ftok)
                    kreg_free_tok = ("pe", C("pe"), "tensor", False)
                    P.need("tensor", ("dv", C("dv"), "vector", False))
                    P.need("tensor", cw_done)
                    P.need("tensor", tok_wo)
                    for oc in range(NIC):
                        bank = oc % 2
                        P.acquire("tensor", BK(bank))
                        for ic in range(NIC):
                            I("tensor", "matmul", pb[bank][:],
                              wstgB[:, ic, oc * 128:(oc + 1) * 128],
                              attn_fmb[:, ic, :], start=(ic == 0), stop=False)
                        mm = I("tensor", "matmul", pb[bank][:],
                               b2bf[0:E, oc * 128:(oc + 1) * 128],
                               cw_fm[0:E, tqb * 512:(tqb + 1) * 512], start=False,
                               stop=True, inc="pe")
                        P.need("scalar", mm)
                        P.acquire("scalar", "out_acc_w")
                        stok = I("scalar", "activation",
                                 out=out_acc[:, oc, tqb * 512:(tqb + 1) * 512],
                                 in_=pb[bank][:], func=AF.Identity,
                                 bias=bopt[:, oc:oc + 1], inc="sc")
                        P.freed(BK(bank), stok)
                    P.freed("attn_fmb", ("pe", C("pe"), "tensor", False))
                wo_all_done = ("sc", C("sc"), "scalar", False)
                ck(f"attn{s}{th}")
                attn_pe_done = ("pe", C("pe"), "tensor", False)
                P.freed("wstgB", attn_pe_done)
                P.freed("kreg", kreg_free_tok)
                P.freed("sqbuf", attn_pe_done)

                for e in range(E):
                    for half in range(2):
                        P.acquire("sync", "cwrow")
                        P.need("sync", cw_done)
                        P.need("sync", attn_pe_done)
                        W("sync", "scw", 16 * ncw[0])
                        ncw[0] += 1
                        I("sync", "dma_start", out=cwrow,
                          in_=cw_fm[e:e + 1, half * 512:(half + 1) * 512],
                          inc="scw", by=16)
                        cwr_tok = ("scw", 16 * ncw[0], "sync", True)
                        P.need("tensor", cwr_tok)
                        P.acquire("tensor", BK(6))
                        mm = I("tensor", "matmul", pb[6][:], onesb[0:1, 0:128],
                               cwrow[0:1, :], start=True, stop=True, inc="pe")
                        P.freed("cwrow", ("pe", C("pe"), "tensor", False))
                        P.need("vector", mm)
                        vtok = I("vector", "tensor_copy",
                                 out=cwb[:, half * 512:(half + 1) * 512],
                                 in_=pb[6][:], inc="dv")
                        P.freed(BK(6), vtok)
                    cwb_tok = ("dv", C("dv"), "vector", False)

                    for hb in range(NHB):
                        f = e * NHB + hb
                        sl = (f + 1) % 2
                        if sl == 0:
                            P.acquire("gpsimd", "wstgA")
                            P.acquire("gpsimd", "wstgB")
                        else:
                            P.acquire("gpsimd", "kreg")
                        W("gpsimd", S_SW[sl], 16 * nsw[sl])
                        nsw[sl] += 1
                        I("gpsimd", "dma_start", out=w1S[sl][:, :, :],
                          in_=d_w1T[e, :, hb * HB:(hb + 1) * HB].rearrange(
                              "(c p) n -> p c n", p=128), inc=S_SW[sl], by=16)
                        nsw[sl] += 1
                        I("gpsimd", "dma_start", out=w2S[sl][:, :, :],
                          in_=d_w2T[e, hb * HB:(hb + 1) * HB, :].rearrange(
                              "(c p) n -> p c n", p=128), inc=S_SW[sl], by=16)
                        wtok = (S_SW[sl], 16 * nsw[sl], "gpsimd", True)
                        P.need("tensor", wtok)
                        for tb in range(NTB):
                            gtoks = []
                            for hc in range(HCB):
                                hbank = hc % 2
                                P.acquire("tensor", BK(hbank))
                                for ic in range(NIC):
                                    mm = I("tensor", "matmul", pb[hbank][:],
                                           w1S[sl][:, ic, hc * 128:(hc + 1) * 128],
                                           xn_fm[:, ic, qb + tb * TB: qb + (tb + 1) * TB],
                                           start=(ic == 0), stop=(ic == NIC - 1),
                                           inc=("pe" if ic == NIC - 1 else None))
                                P.need("scalar", mm)
                                if hc == 0:
                                    P.acquire("scalar", "h_all")
                                gtok = I("scalar", "activation", out=h_all[:, hc, :],
                                         in_=pb[hbank][:], func=AF.Gelu_apprx_tanh,
                                         bias=b1pt[:, e, hb * HCB + hc: hb * HCB + hc + 1],
                                         inc="sc")
                                gtoks.append(gtok)
                                P.freed(BK(hbank), gtok)
                            P.need("tensor", gtoks[-1])
                            for half in range(2):
                                for hc in range(HCB):
                                    for oc4 in range(4):
                                        oc = half * 4 + oc4
                                        obank = 2 + oc4
                                        if hc == 0:
                                            P.acquire("tensor", BK(obank))
                                        mm2 = I("tensor", "matmul", pb[obank][:],
                                                w2S[sl][:, hc, oc * 128:(oc + 1) * 128],
                                                h_all[:, hc, :], start=(hc == 0),
                                                stop=(hc == HCB - 1),
                                                inc=("pe" if (hc == HCB - 1 and oc4 == 3)
                                                     else None))
                                P.need("vector", mm2)
                                W("vector", "dv", cwb_tok[1])
                                P.need("vector", wo_all_done)
                                for oc4 in range(4):
                                    vtok = I("vector", "tensor_tensor",
                                             out=pb[2 + oc4][:], in0=pb[2 + oc4][:],
                                             in1=cwb[:, tb * TB:(tb + 1) * TB],
                                             op=ALU.mult, inc="dv")
                                W("vector", "dv", vtok[1])
                                for oc4 in range(4):
                                    oc = half * 4 + oc4
                                    atok = I("vector", "tensor_tensor",
                                      out=out_acc[:, oc, tb * TB:(tb + 1) * TB],
                                      in0=out_acc[:, oc, tb * TB:(tb + 1) * TB],
                                      in1=pb[2 + oc4][:], op=ALU.add,
                                      inc="dv")
                                    P.freed(BK(2 + oc4), atok)
                            P.freed("h_all", ("pe", C("pe"), "tensor", False))
                        consumed = ("pe", C("pe"), "tensor", False)
                        if sl == 0:
                            P.freed("wstgA", consumed)
                            P.freed("wstgB", consumed)
                        else:
                            P.freed("kreg", consumed)
                ffn_done = ("dv", C("dv"), "vector", False)
                ck(f"ffn{s}{th}")

                for tc in range(TH // 128):
                    p = tc % 2
                    P.acquire("sync", f"x_stage{p}")
                    P.need("sync", attn_pe_done)
                    W("sync", S_SX[p], 16 * nx[p])
                    nx[p] += 1
                    I("sync", "dma_start", out=x_stage[:, p, :],
                      in_=d_x[s, qb + tc * 128: qb + (tc + 1) * 128, :], inc=S_SX[p], by=16)
                    xtok = (S_SX[p], 16 * nx[p], "sync", True)
                    P.need("tensor", ffn_done)
                    vtok = None
                    for oc in range(NIC):
                        bank = 6 + oc % 2
                        P.acquire("tensor", BK(bank))
                        ttok = I("tensor", "transpose", pb[bank][:, 0:128],
                                 out_acc[:, oc, tc * 128:(tc + 1) * 128], identf[:],
                                 inc="pe")
                        P.need("vector", ttok)
                        P.need("vector", xtok)
                        if oc == 0:
                            P.acquire("vector", f"out_tm{p}")
                        vtok = I("vector", "tensor_tensor",
                                 out=out_tm[:, p, oc * 128:(oc + 1) * 128],
                                 in0=pb[bank][:, 0:128],
                                 in1=x_stage[:, p, oc * 128:(oc + 1) * 128],
                                 op=ALU.add, inc="dv")
                        P.freed(BK(bank), vtok)
                        if oc == NIC - 1:
                            P.freed(f"x_stage{p}", vtok)
                    P.need("sync", vtok)
                    W("sync", S_ST[p], 16 * nst[p])
                    nst[p] += 1
                    I("sync", "dma_start",
                      out=d_out[s, qb + tc * 128: qb + (tc + 1) * 128, :],
                      in_=out_tm[:, p, :], inc=S_ST[p], by=16)
                    P.freed(f"out_tm{p}", (S_ST[p], 16 * nst[p], "sync", True))
                P.freed("out_acc_w", ("pe", C("pe"), "tensor", False))
                ck(f"final{s}{th}")

        except _StopPlan:
            pass

        if upto is not None:
            for sname in ["pe", "sc", "dv", "sg"]:
                W("sync", sname, C(sname))
            dumps = [("dbg_xn", xn_fm[:, :, :], [128, NIC, S], BF16),
                     ("dbg_q", q_fm, [128, NIC, TH], BF16),
                     ("dbg_k", k_fm, [128, NIC, S], BF16),
                     ("dbg_v", v_tm, [128, S // 128, H * 65], BF16),
                     ("dbg_cw", cw_fm[:, :], [E, TH], BF16),
                     ("dbg_acc", out_acc[:, :, :], [128, NIC, TH], F32)]
            for nm, src, shp, dt in dumps:
                dd = nc.dram_tensor(nm, shp, dt, kind="ExternalOutput").ap()
                nst[0] += 1
                I("sync", "dma_start", out=dd, in_=src, inc=S_ST[0], by=16)

        W("sync", S_ST[0], 16 * nst[0])
        W("sync", S_ST[1], 16 * nst[1])

        for e_ in ENGS:
            for op in P.ops[e_]:
                if op[0] == "w":
                    assert op[2] <= C(op[1]), (e_, op, C(op[1]))

        slopes = _slopes(H)

        def replay(eng_name, eng):
            for op in P.ops[eng_name]:
                if op[0] == "w":
                    eng.wait_ge(sems[op[1]], op[2])
                else:
                    _, method, args, kw, inc, by = op
                    kw2 = {}
                    for k, v in kw.items():
                        if isinstance(v, tuple) and len(v) == 2 and v[0] == "NEGSLOPE":
                            v = float(-slopes[v[1]])
                        kw2[k] = v
                    inst = getattr(eng, method)(*args, **kw2)
                    if inc:
                        inst.then_inc(sems[inc], by)

        block = es.enter_context(nc.Block())

        @block.sync
        def _(eng):
            replay("sync", eng)

        @block.gpsimd
        def _(eng):
            replay("gpsimd", eng)

        @block.tensor
        def _(eng):
            replay("tensor", eng)

        @block.scalar
        def _(eng):
            replay("scalar", eng)

        @block.vector
        def _(eng):
            replay("vector", eng)

    return nc


def _prep_inputs(ins):
    f = np.float32
    slopes = _slopes(H)
    wqT = np.ascontiguousarray(ins["L0_attn_Wq"].T)
    wkT = np.ascontiguousarray(ins["L0_attn_Wk"].T)
    wvT = np.ascontiguousarray(ins["L0_attn_Wv"].T)
    woT = np.ascontiguousarray(ins["L0_attn_Wo"].T)
    w1T = np.ascontiguousarray(ins["L0_ffn_w1"].transpose(0, 2, 1))
    w2T = np.ascontiguousarray(ins["L0_ffn_w2"].transpose(0, 2, 1))
    rwT = np.ascontiguousarray(ins["L0_ffn_router_w"].T)
    bqpt = np.ascontiguousarray(ins["L0_attn_bq"].reshape(NIC, 128).T)
    bkpt = np.ascontiguousarray(ins["L0_attn_bk"].reshape(NIC, 128).T)
    bopt = np.ascontiguousarray(ins["L0_attn_bo"].reshape(NIC, 128).T)
    bvr = np.ascontiguousarray(ins["L0_attn_bv"].reshape(1, D))
    b1pt = np.ascontiguousarray(
        ins["L0_ffn_b1"].reshape(E, HID // 128, 128).transpose(2, 0, 1))
    b2 = np.ascontiguousarray(ins["L0_ffn_b2"])
    identb = np.eye(128).astype(ml_dtypes.bfloat16)
    identf = np.eye(128, dtype=f)
    jj, pp = np.meshgrid(np.arange(S // 128, dtype=f), np.arange(128, dtype=f))
    rowshift = np.zeros((128, H * (S // 128)), f)
    for h in range(H):
        rowshift[:, h * 16:(h + 1) * 16] = slopes[h] * (128.0 * jj + pp)
    minf = np.tril(np.full((128, 128), -1e30, f), -1)
    tqrow = np.arange(S, dtype=f).reshape(1, S)
    shared = dict(wqT=wqT, wkT=wkT, wvT=wvT, woT=woT, w1T=w1T, w2T=w2T, rwT=rwT,
                  bqpt=bqpt, bkpt=bkpt, bopt=bopt, bvr=bvr, b1pt=b1pt, b2=b2,
                  identb=identb, identf=identf, rowshift=rowshift, minf=minf,
                  tqrow=tqrow)
    x = np.ascontiguousarray(ins["L0_attn_x"])
    in_maps = []
    for c in range(NCORE):
        m = dict(shared)
        m["x"] = np.ascontiguousarray(x[c * BPC:(c + 1) * BPC])
        in_maps.append(m)
    return in_maps


def kernel(**ins):
    nc = build_core_program()
    in_maps = _prep_inputs(ins)
    res = run_bass_kernel_spmd(nc, in_maps, core_ids=list(range(NCORE)))
    return np.concatenate([r["out"] for r in res.results], axis=0)


# version 17 (final)
# speedup vs baseline: 1.1980x; 1.1887x over previous
import math
from contextlib import ExitStack

import numpy as np
import ml_dtypes

import concourse.bass as bass
import concourse.mybir as mybir
from concourse.bass_utils import run_bass_kernel_spmd

F32 = mybir.dt.float32
BF16 = mybir.dt.bfloat16
AF = mybir.ActivationFunctionType
ALU = mybir.AluOpType
AX = mybir.AxisListType

B, S, D = 16, 2048, 1024
NCORE = 8
BPC = B // NCORE
H, HD = 16, 64
E, HID = 8, 4096
NIC = D // 128
NTH = 2
TH = S // NTH
HB = 1024
NHB = HID // HB
HCB = HB // 128
TB = 512
NTB = TH // TB

ENGS = ["tensor", "scalar", "vector", "gpsimd", "sync"]


def _slopes(n):
    start = 2.0 ** (-(2.0 ** (-(math.log2(n) - 3))))
    return np.array([start * start ** i for i in range(n)], dtype=np.float32)


class Plan:

    def __init__(self):
        self.ops = {e: [] for e in ENGS}
        self.cnt = {}
        self.emitted = {e: {} for e in ENGS}
        self.gate = {}

    def C(self, sem):
        return self.cnt.get(sem, 0)

    def I(self, eng, method, *args, inc=None, by=1, **kw):
        self.ops[eng].append(("i", method, args, kw, inc, by))
        if inc:
            self.cnt[inc] = self.cnt.get(inc, 0) + by
            return (inc, self.cnt[inc], eng, by >= 16)
        return None

    def W(self, eng, sem, val):
        if val <= 0:
            return
        if self.emitted[eng].get(sem, -1) >= val:
            return
        self.emitted[eng][sem] = val
        self.ops[eng].append(("w", sem, val))

    def need(self, eng, tok):
        if tok is None:
            return
        sem, val, src_eng, is_dma = tok
        if (not is_dma) and src_eng == eng:
            return
        self.W(eng, sem, val)

    def freed(self, key, tok):
        self.gate[key] = tok

    def acquire(self, eng, key):
        self.need(eng, self.gate.get(key))


class _StopPlan(Exception):
    pass


def build_core_program(upto=None):
    nc = bass.Bass("TRN2", target_bir_lowering=False)

    d_x = nc.dram_tensor("x", [BPC, S, D], F32, kind="ExternalInput").ap()
    d_out = nc.dram_tensor("out", [BPC, S, D], F32, kind="ExternalOutput").ap()
    d_wqT = nc.dram_tensor("wqT", [D, D], F32, kind="ExternalInput").ap()
    d_wkT = nc.dram_tensor("wkT", [D, D], F32, kind="ExternalInput").ap()
    d_wvT = nc.dram_tensor("wvT", [D, D], F32, kind="ExternalInput").ap()
    d_woT = nc.dram_tensor("woT", [D, D], F32, kind="ExternalInput").ap()
    d_w1T = nc.dram_tensor("w1T", [E, D, HID], F32, kind="ExternalInput").ap()
    d_w2T = nc.dram_tensor("w2T", [E, HID, D], F32, kind="ExternalInput").ap()
    d_rwT = nc.dram_tensor("rwT", [D, E], F32, kind="ExternalInput").ap()
    d_bqpt = nc.dram_tensor("bqpt", [128, NIC], F32, kind="ExternalInput").ap()
    d_bkpt = nc.dram_tensor("bkpt", [128, NIC], F32, kind="ExternalInput").ap()
    d_bopt = nc.dram_tensor("bopt", [128, NIC], F32, kind="ExternalInput").ap()
    d_bv = nc.dram_tensor("bvr", [1, D], F32, kind="ExternalInput").ap()
    d_b1pt = nc.dram_tensor("b1pt", [128, E, HID // 128], F32, kind="ExternalInput").ap()
    d_b2 = nc.dram_tensor("b2", [E, D], F32, kind="ExternalInput").ap()
    d_identb = nc.dram_tensor("identb", [128, 128], BF16, kind="ExternalInput").ap()
    d_identf = nc.dram_tensor("identf", [128, 128], F32, kind="ExternalInput").ap()
    d_rowshift = nc.dram_tensor("rowshift", [128, H * (S // 128)], F32, kind="ExternalInput").ap()
    d_minf = nc.dram_tensor("minf", [128, 128], F32, kind="ExternalInput").ap()
    d_tq = nc.dram_tensor("tqrow", [1, S], F32, kind="ExternalInput").ap()

    es = ExitStack()
    sb = lambda name, shape, dt: es.enter_context(nc.sbuf_tensor("s_" + name, shape, dt))
    psal = lambda name, shape, dt: es.enter_context(nc.psum_tensor(name, shape, dt))
    mksem = lambda name: es.enter_context(nc.semaphore(name))

    with es:
        identb = sb("identb", [128, 128], BF16)
        identf = sb("identf", [128, 128], F32)
        rowshift = sb("rowshift", [128, H * (S // 128)], F32)
        minf = sb("minf", [128, 128], F32)
        bqpt = sb("bqpt", [128, NIC], F32)
        bkpt = sb("bkpt", [128, NIC], F32)
        bopt = sb("bopt", [128, NIC], F32)
        bq8 = sb("bq8", [128, NIC], F32)
        b1pt = sb("b1pt", [128, E, HID // 128], F32)
        b2bf = sb("b2bf", [E, D], BF16)
        rwT = sb("rwT", [128, NIC, E], BF16)
        bvrow = sb("bvrow", [1, D], BF16)
        tqbf = sb("tqbf", [1, S], BF16)
        onesb = sb("onesb", [1, 128], BF16)
        onesf = sb("onesf", [1, 128], F32)
        csh = sb("csh", [1, 512], BF16)
        epsv = sb("epsv", [128, 1], F32)
        xn_fm = sb("xn_fm", [128, NIC, S], BF16)
        out_acc = sb("out_acc", [128, NIC, TH], F32)
        cw_fm = sb("cw_fm", [E, TH], BF16)

        AR_N = 63744
        arena = sb("arena", [128, AR_N], BF16)
        arb = arena.ap()
        arf = arena.ap().bitcast(F32)

        def bfv(off, shape):
            n = int(np.prod(shape))
            apx = arb[:, off:off + n]
            if len(shape) == 1:
                return apx
            if len(shape) == 2:
                return apx.rearrange("p (a b) -> p a b", b=shape[1])
            return apx.rearrange("p (a b c) -> p a b c", b=shape[1], c=shape[2])

        def f32v(off_bf, shape):
            off = off_bf // 2
            n = int(np.prod(shape))
            apx = arf[:, off:off + n]
            if len(shape) == 1:
                return apx
            return apx.rearrange("p (a b) -> p a b", b=shape[1])

        q_fm = bfv(0, (NIC, TH))
        k_fm = bfv(8192, (NIC, S))
        v_tm = bfv(24576, (S // 128, H * 65))
        wstgA = bfv(41216, (NIC, D))
        wstgB = bfv(49408, (NIC, D))
        attn_fmb = bfv(57600, (NIC, 512))
        p_t = bfv(61696, (4, 512))
        cwrow = arb[0:1, 32768:33280]
        w1A = bfv(41216, (NIC, HB))
        w2A = bfv(49408, (HCB, D))
        w1B = bfv(8192, (NIC, HB))
        w2B = bfv(16384, (HCB, D))
        w1S, w2S = [w1A, w1B], [w2A, w2B]
        h_all = bfv(24576, (HCB, 512))
        cwb = bfv(36864, (D,))
        sqbuf = f32v(45312, (D,))
        rrow = f32v(43264, (512,))
        rrowB = f32v(46336, (512,))
        out_tm = f32v(0, (2, D))
        x_stage = f32v(4096, (2, D))
        xn_tm = bfv(41216, (2, D))
        rp = f32v(44288, (E,))
        probs = f32v(44304, (E,))
        eq1 = f32v(44320, (E,))
        t1v = f32v(44336, (E,))
        pwm = f32v(44352, (E,))
        t2v = f32v(44368, (E,))
        keep = f32v(44384, (E,))
        m1 = f32v(44400, (1,))
        m2 = f32v(44408, (1,))
        den = f32v(44416, (1,))
        rd = f32v(44424, (1,))
        ssum = f32v(44432, (1,))
        rsum = f32v(44440, (1,))
        bn6 = f32v(44448, (12,))
        m2v = f32v(44480, (1,))
        varv = f32v(44488, (1,))
        mv = f32v(44496, (2,))
        rshadj = f32v(44528, (16,))
        rshadjB = f32v(44576, (16,))
        cshB = arb[0:1, 44640:45152]
        sqv = f32v(44504, (1,))
        rstd = f32v(44512, (1,))
        mrs = f32v(44520, (1,))

        pb = [psal(f"pb{i}", [128, 512], F32) for i in range(8)]
        pbb = [p.ap().bitcast(BF16) for p in pb]

        SEM_NAMES = ["ld", "ldg", "sg", "sx0", "sx1", "sw0", "sw1", "st0", "st1",
                     "scw", "pe", "sc", "dv"]
        sems = {n: mksem(n) for n in SEM_NAMES}
        S_SX, S_SW, S_ST = ["sx0", "sx1"], ["sw0", "sw1"], ["st0", "st1"]

        P = Plan()
        I, W, C = P.I, P.W, P.C
        BK = lambda i: f"bank{i}"

        def ck(name):
            if upto is not None and upto == name:
                raise _StopPlan()

        def vch(method, *args, **kw):
            tk = I("vector", method, *args, inc="dv", **kw)
            W("vector", "dv", tk[1])
            return tk

        for dst, src in [(identb, d_identb), (identf, d_identf), (rowshift, d_rowshift),
                         (minf, d_minf), (bqpt, d_bqpt), (bkpt, d_bkpt), (bopt, d_bopt),
                         (b1pt, d_b1pt)]:
            I("sync", "dma_start", out=dst[:], in_=src[:], inc="ld", by=16)
        n_const = C("ld")
        for dst, src in [(b2bf, d_b2), (bvrow, d_bv), (tqbf, d_tq)]:
            I("gpsimd", "dma_start", out=dst[:], in_=src[:], inc="ldg", by=16)
        I("gpsimd", "dma_start", out=rwT[:], in_=d_rwT.rearrange("(c p) e -> p c e", p=128),
          inc="ldg", by=16)
        n_constg = C("ldg")
        I("gpsimd", "memset", onesb[:], 1.0)
        I("gpsimd", "memset", onesf[:], 1.0, inc="sg")
        n_sg0 = C("sg")
        for e_ in ["tensor", "scalar", "vector", "gpsimd"]:
            W(e_, "ld", n_const)
            W(e_, "ldg", n_constg)
            W(e_, "sg", n_sg0)
        I("vector", "memset", epsv[:], 1e-5)
        tok_bq8 = I("vector", "tensor_scalar", out=bq8[:], in0=bqpt[:], scalar1=0.125,
                    scalar2=None, op0=ALU.mult, inc="dv")

        nx = [0, 0]
        nst = [0, 0]
        ncw = [0]
        nsw = [0, 0]

        try:
          for s in range(BPC):
            W("vector", "pe", C("pe"))
            for tc in range(S // 128):
                p = tc % 2
                P.acquire("sync", f"x_stage{p}")
                W("sync", S_SX[p], 16 * nx[p])
                nx[p] += 1
                I("sync", "dma_start", out=x_stage[:, p, :],
                  in_=d_x[s, tc * 128:(tc + 1) * 128, :], inc=S_SX[p], by=16)
                xtok = (S_SX[p], 16 * nx[p], "sync", True)
                P.need("vector", xtok)
                P.need("scalar", xtok)
                tok = vch("reduce_sum", mv[:, 0:1], x_stage[:, p, :], axis=AX.X)
                P.acquire("scalar", "sqbuf")
                stok = I("scalar", "activation", out=sqbuf, in_=x_stage[:, p, :],
                         func=AF.Square, inc="sc")
                P.need("vector", stok)
                tok2 = vch("reduce_sum", mv[:, 1:2], sqbuf, axis=AX.X)
                P.freed("sqbuf", tok2)
                vch("tensor_scalar", out=mv, in0=mv, scalar1=1.0 / D,
                    scalar2=None, op0=ALU.mult)
                vch("tensor_tensor", out=m2v, in0=mv[:, 0:1], in1=mv[:, 0:1],
                    op=ALU.mult)
                tok = vch("tensor_tensor", out=varv, in0=mv[:, 1:2], in1=m2v,
                          op=ALU.subtract)
                P.need("scalar", tok)
                tok = I("scalar", "activation", out=sqv, in_=varv, func=AF.Sqrt,
                        bias=epsv[:], scale=1.0, inc="sc")
                P.need("vector", tok)
                vch("reciprocal", rstd, sqv)
                vch("tensor_tensor", out=mrs, in0=mv[:, 0:1], in1=rstd, op=ALU.mult)
                P.acquire("vector", f"xn_tm{p}")
                tok_xn = I("vector", "tensor_scalar", out=xn_tm[:, p, :], in0=x_stage[:, p, :],
                           scalar1=rstd, scalar2=mrs, op0=ALU.mult, op1=ALU.subtract,
                           inc="dv")
                P.freed(f"x_stage{p}", tok_xn)
                P.need("tensor", tok_xn)
                for oc in range(NIC):
                    bank = 6 + oc % 2
                    P.acquire("tensor", BK(bank))
                    ttok = I("tensor", "transpose", pbb[bank][:, 0:128],
                             xn_tm[:, p, oc * 128:(oc + 1) * 128], identb[:], inc="pe")
                    P.need("vector", ttok)
                    vtok = I("vector", "tensor_copy", out=xn_fm[:, oc, tc * 128:(tc + 1) * 128],
                             in_=pbb[bank][:, 0:128], inc="dv")
                    P.freed(BK(bank), vtok)
                    if oc == NIC - 1:
                        P.freed(f"xn_tm{p}", ttok)
            xn_done = ("dv", C("dv"), "vector", False)
            P.freed("wstgA", ("pe", C("pe"), "tensor", False))
            ck(f"ln{s}")

            for th in range(NTH):
                qb = th * TH
                kv = (th + 1) * TH
                nkc = kv // 128

                def stg_fill(slot_i, wsrc):
                    P.acquire("gpsimd", "wstgA" if slot_i == 0 else "wstgB")
                    W("gpsimd", S_SW[slot_i], 16 * nsw[slot_i])
                    nsw[slot_i] += 1
                    I("gpsimd", "dma_start", out=(wstgA if slot_i == 0 else wstgB)[:, :, :],
                      in_=wsrc.rearrange("(c p) n -> p c n", p=128), inc=S_SW[slot_i], by=16)
                    return (S_SW[slot_i], 16 * nsw[slot_i], "gpsimd", True)

                tok_wq = stg_fill(0, d_wqT)
                tok_wk = stg_fill(1, d_wkT)

                W("scalar", S_SX[0], 16 * nx[0])
                W("scalar", S_SX[1], 16 * nx[1])
                W("scalar", S_ST[0], 16 * nst[0])
                W("scalar", S_ST[1], 16 * nst[1])
                W("scalar", "dv", C("dv"))

                P.need("tensor", xn_done)
                P.need("tensor", tok_wq)
                P.need("scalar", tok_bq8)
                for u, (tqb, oc) in enumerate([(a, b) for a in range(TH // 512)
                                               for b in range(NIC)]):
                    bank = u % 4
                    P.acquire("tensor", BK(bank))
                    for ic in range(NIC):
                        mm = I("tensor", "matmul", pb[bank][:],
                               wstgA[:, ic, oc * 128:(oc + 1) * 128],
                               xn_fm[:, ic, qb + tqb * 512: qb + (tqb + 1) * 512],
                               start=(ic == 0), stop=(ic == NIC - 1),
                               inc=("pe" if ic == NIC - 1 else None))
                    P.need("scalar", mm)
                    stok = I("scalar", "activation", out=q_fm[:, oc, tqb * 512:(tqb + 1) * 512],
                             in_=pb[bank][:], func=AF.Identity, bias=bq8[:, oc:oc + 1],
                             scale=0.125, inc="sc")
                    P.freed(BK(bank), stok)
                P.freed("wstgA", ("pe", C("pe"), "tensor", False))

                P.need("tensor", tok_wk)
                P.acquire("scalar", "kreg")
                for u, (kb, oc) in enumerate([(a, b) for a in range(kv // 512)
                                              for b in range(NIC)]):
                    bank = u % 4
                    P.acquire("tensor", BK(bank))
                    for ic in range(NIC):
                        mm = I("tensor", "matmul", pb[bank][:],
                               wstgB[:, ic, oc * 128:(oc + 1) * 128],
                               xn_fm[:, ic, kb * 512:(kb + 1) * 512],
                               start=(ic == 0), stop=(ic == NIC - 1),
                               inc=("pe" if ic == NIC - 1 else None))
                    P.need("scalar", mm)
                    stok = I("scalar", "activation", out=k_fm[:, oc, kb * 512:(kb + 1) * 512],
                             in_=pb[bank][:], func=AF.Identity, bias=bkpt[:, oc:oc + 1],
                             scale=1.0, inc="sc")
                    P.freed(BK(bank), stok)
                P.freed("wstgB", ("pe", C("pe"), "tensor", False))
                tok_wv = stg_fill(0, d_wvT)
                tok_wo = stg_fill(1, d_woT)

                W("gpsimd", "pe", C("pe"))
                W("gpsimd", "dv", C("dv"))
                W("gpsimd", "sc", C("sc"))
                W("gpsimd", "scw", 16 * ncw[0])
                for tkc in range(nkc):
                    ocols = v_tm[:, tkc, :].rearrange("p (h d) -> p h d", d=65)[:, :, 64:65]
                    I("gpsimd", "memset", ocols, 1.0,
                      inc=("sg" if tkc == nkc - 1 else None))
                ones_ready = ("sg", C("sg"), "gpsimd", False)

                P.need("tensor", tok_wv)
                for u, (tkc, oh) in enumerate([(a, b) for a in range(nkc) for b in range(2)]):
                    bank = u % 4
                    P.acquire("tensor", BK(bank))
                    for ic in range(NIC):
                        I("tensor", "matmul", pb[bank][:],
                          xn_fm[:, ic, tkc * 128:(tkc + 1) * 128],
                          wstgA[:, ic, oh * 512:(oh + 1) * 512],
                          start=(ic == 0), stop=False)
                    mm = I("tensor", "matmul", pb[bank][:], onesb[0:1, 0:128],
                           bvrow[0:1, oh * 512:(oh + 1) * 512], start=False, stop=True,
                           inc="pe")
                    P.need("vector", mm)
                    P.need("vector", ones_ready)
                    dst = v_tm[:, tkc, oh * 520:(oh + 1) * 520].rearrange(
                        "p (h d) -> p h d", d=65)[:, :, 0:64]
                    src = pb[bank].ap().rearrange("p (h d) -> p h d", d=64)
                    vtok = I("vector", "tensor_copy", out=dst, in_=src, inc="dv")
                    P.freed(BK(bank), vtok)
                P.freed("wstgA", ("pe", C("pe"), "tensor", False))

                for tc in range(TH // 128):
                    P.acquire("tensor", BK(6))
                    for ic in range(NIC):
                        mm = I("tensor", "matmul", pb[6][:, 0:E],
                               xn_fm[:, ic, qb + tc * 128: qb + (tc + 1) * 128],
                               rwT[:, ic, :], start=(ic == 0), stop=(ic == NIC - 1),
                               inc=("pe" if ic == NIC - 1 else None))
                    P.need("scalar", mm)
                    stok = I("scalar", "activation", out=rp, in_=pb[6][:, 0:E],
                             func=AF.Exp, inc="sc")
                    P.freed(BK(6), stok)
                    P.need("vector", stok)
                    vch("reduce_sum", ssum, rp, axis=AX.X)
                    vch("reciprocal", rsum, ssum)
                    vch("tensor_scalar", out=probs, in0=rp, scalar1=rsum,
                        scalar2=None, op0=ALU.mult)
                    vch("reduce_max", m1, probs, axis=AX.X)
                    vch("tensor_tensor", out=eq1, in0=probs,
                        in1=m1.broadcast_to((128, E)), op=ALU.is_equal)
                    vch("tensor_tensor", out=t1v, in0=eq1,
                        in1=m1.broadcast_to((128, E)), op=ALU.mult)
                    vch("tensor_tensor", out=pwm, in0=probs, in1=t1v, op=ALU.subtract)
                    vch("reduce_max", m2, pwm, axis=AX.X)
                    vch("tensor_tensor", out=keep, in0=pwm,
                        in1=m2.broadcast_to((128, E)), op=ALU.is_equal)
                    vch("tensor_tensor", out=keep, in0=eq1, in1=keep, op=ALU.add)
                    vch("tensor_tensor", out=den, in0=m1, in1=m2, op=ALU.add)
                    vch("tensor_scalar", out=den, in0=den, scalar1=1e-8,
                        scalar2=None, op0=ALU.add)
                    vch("reciprocal", rd, den)
                    vch("tensor_tensor", out=t2v, in0=probs, in1=keep, op=ALU.mult)
                    dtok = vch("tensor_scalar", out=rp, in0=t2v, scalar1=rd,
                               scalar2=None, op0=ALU.mult)
                    P.need("tensor", dtok)
                    P.acquire("tensor", BK(7))
                    ttok = I("tensor", "transpose", pb[7][0:E, 0:128], rp, identf[:],
                             inc="pe")
                    P.need("vector", ttok)
                    vtok = I("vector", "tensor_copy", out=cw_fm[0:E, tc * 128:(tc + 1) * 128],
                             in_=pb[7][0:E, 0:128], inc="dv")
                    P.freed(BK(7), vtok)
                cw_done = ("dv", C("dv"), "vector", False)
                ck(f"router{s}{th}")

                W("tensor", "sc", C("sc"))
                W("tensor", "dv", C("dv"))
                W("tensor", "scw", 16 * ncw[0])
                W("vector", "scw", 16 * ncw[0])
                P.need("tensor", ones_ready)
                CUT = 110.0
                slopes_py = _slopes(H)

                def mk_tiles(gq0, h):
                    tl = []
                    for j in range(gq0 + 1):
                        dmin = 128 * gq0 - (128 * j + 127)
                        if j < gq0 and slopes_py[h] * dmin > CUT:
                            continue
                        tl.append((j, 0, 512, j == gq0))
                    for qt in range(1, 4):
                        for j in range(gq0 + 1, gq0 + qt + 1):
                            dmin = 128 * (gq0 + qt) - (128 * j + 127)
                            if j < gq0 + qt and slopes_py[h] * dmin > CUT:
                                continue
                            tl.append((j, qt * 128, 128, j == gq0 + qt))
                    return tl

                for tqb in range(TH // 512):
                    gq0 = th * 8 + tqb * 4
                    for hp2 in range(H // 2):
                        chains = []
                        for c in range(2):
                            h = 2 * hp2 + c
                            skip_cs = slopes_py[h] * 258.0 <= 45.0
                            csh_c = csh[:] if c == 0 else cshB
                            radj = rshadj if c == 0 else rshadjB
                            if skip_cs:
                                P.acquire("vector", f"rshadj{c}")
                                cst = I("vector", "tensor_scalar", out=radj,
                                        in0=rowshift[:, h * 16:(h + 1) * 16],
                                        scalar1=float(-slopes_py[h] *
                                                      (gq0 * 128 + 255.5)),
                                        scalar2=None, op0=ALU.add, inc="dv")
                            else:
                                P.acquire("vector", f"csh{c}")
                                cst = I("vector", "tensor_scalar", out=csh_c,
                                        in0=tqbf[0:1, gq0 * 128: gq0 * 128 + 512],
                                        scalar1=("NEGSLOPE", h), scalar2=None,
                                        op0=ALU.mult, inc="dv")
                            chains.append(dict(
                                h=h, c=c, skip_cs=skip_cs, csh=csh_c, radj=radj,
                                cst=cst, tiles=mk_tiles(gq0, h),
                                stb=[0, 1] if c == 0 else [6, 7],
                                pvb=2 + c, rbb=4 + c,
                                pslot=[2 * c, 2 * c + 1]))
                        nmax = max(len(ch["tiles"]) for ch in chains)
                        for k in range(nmax):
                            live = [ch for ch in chains if k < len(ch["tiles"])]
                            for ch in live:
                                h, c = ch["h"], ch["c"]
                                hp, hc2 = h % 2, h // 2
                                pbase = 64 * hp
                                j, coff, cwid, diag = ch["tiles"][k]
                                stb = ch["stb"][k % 2]
                                P.need("tensor", ch["cst"])
                                P.acquire("tensor", BK(stb))
                                if not ch["skip_cs"]:
                                    I("tensor", "matmul", pb[stb][:, 0:cwid],
                                      onesb[0:1, 0:128],
                                      (ch["csh"])[0:1, coff:coff + cwid],
                                      start=True, stop=False)
                                mm = I("tensor", "matmul", pb[stb][:, 0:cwid],
                                       k_fm[pbase:pbase + 64, hc2,
                                            j * 128:(j + 1) * 128],
                                       q_fm[pbase:pbase + 64, hc2,
                                            tqb * 512 + coff: tqb * 512 + coff + cwid],
                                       start=ch["skip_cs"], stop=True, inc="pe")
                                ch["mm"] = mm
                            for ch in live:
                                h, c = ch["h"], ch["c"]
                                j, coff, cwid, diag = ch["tiles"][k]
                                stb = ch["stb"][k % 2]
                                exp_src = ch["mm"]
                                if diag:
                                    P.need("vector", ch["mm"])
                                    exp_src = I("vector", "tensor_tensor",
                                                out=pb[stb][:, 0:128],
                                                in0=pb[stb][:, 0:128],
                                                in1=minf[:], op=ALU.add, inc="dv")
                                P.need("scalar", exp_src)
                                slot = ch["pslot"][k % 2]
                                P.acquire("scalar", f"p_t{slot}")
                                bias_ap = (ch["radj"][:, j:j + 1] if ch["skip_cs"]
                                           else rowshift[:, h * 16 + j:
                                                         h * 16 + j + 1])
                                etok = I("scalar", "activation",
                                         out=p_t[:, slot, 0:cwid],
                                         in_=pb[stb][:, 0:cwid], func=AF.Exp,
                                         bias=bias_ap, scale=1.0, inc="sc")
                                P.freed(BK(stb), etok)
                                ch["etok"] = etok
                            for ch in live:
                                h, c = ch["h"], ch["c"]
                                j, coff, cwid, diag = ch["tiles"][k]
                                ntile = len(ch["tiles"])
                                slot = ch["pslot"][k % 2]
                                P.need("tensor", ch["etok"])
                                if k == 0:
                                    P.acquire("tensor", BK(ch["pvb"]))
                                pvtok = I("tensor", "matmul",
                                          pb[ch["pvb"]][0:65, coff:coff + cwid],
                                          v_tm[:, j, 65 * h: 65 * h + 65],
                                          p_t[:, slot, 0:cwid],
                                          start=(k == 0), stop=(k == ntile - 1),
                                          inc="pe")
                                P.freed(f"p_t{slot}", pvtok)
                                ch["pvtok"] = pvtok
                        for ch in chains:
                            h, c = ch["h"], ch["c"]
                            hp, hc2 = h % 2, h // 2
                            pbase = 64 * hp
                            ntile = len(ch["tiles"])
                            pvb, rbb = ch["pvb"], ch["rbb"]
                            P.freed(f"rshadj{c}" if ch["skip_cs"] else f"csh{c}",
                                    ch["pvtok"])
                            P.need("vector", ch["pvtok"])
                            P.acquire("vector", f"rrow{c}")
                            rtok = vch("reciprocal", rrow[0:1, 256 * c:256 * c + 512]
                                       if False else rrow[0:1, :] if c == 0 else
                                       rrowB[0:1, :], pb[pvb][64:65, 0:512])
                            slot = ch["pslot"][ntile % 2]
                            P.acquire("vector", f"p_t{slot}")
                            ctok = vch("tensor_copy", out=p_t[0:64, slot, :],
                                       in_=pb[pvb][0:64, 0:512])
                            P.freed(BK(pvb), ctok)
                            P.need("tensor", rtok)
                            P.acquire("tensor", BK(rbb))
                            rbmm = I("tensor", "matmul", pb[rbb][:],
                                     onesf[0:1, 0:128],
                                     rrow[0:1, :] if c == 0 else rrowB[0:1, :],
                                     start=True, stop=True, inc="pe")
                            P.freed(f"rrow{c}", rbmm)
                            P.need("vector", rbmm)
                            P.acquire("vector", "attn_fmb")
                            ftok = I("vector", "tensor_tensor",
                                     out=attn_fmb[pbase:pbase + 64, hc2, :],
                                     in0=p_t[0:64, slot, :],
                                     in1=pb[rbb][0:64, :], op=ALU.mult, inc="dv")
                            P.freed(BK(rbb), ftok)
                            P.freed(f"p_t{slot}", ftok)
                    P.need("tensor", ("dv", C("dv"), "vector", False))
                    P.need("tensor", cw_done)
                    P.need("tensor", tok_wo)
                    for oc in range(NIC):
                        bank = oc % 2
                        P.acquire("tensor", BK(bank))
                        for ic in range(NIC):
                            I("tensor", "matmul", pb[bank][:],
                              wstgB[:, ic, oc * 128:(oc + 1) * 128],
                              attn_fmb[:, ic, :], start=(ic == 0), stop=False)
                        mm = I("tensor", "matmul", pb[bank][:],
                               b2bf[0:E, oc * 128:(oc + 1) * 128],
                               cw_fm[0:E, tqb * 512:(tqb + 1) * 512], start=False,
                               stop=True, inc="pe")
                        P.need("scalar", mm)
                        P.acquire("scalar", "out_acc_w")
                        stok = I("scalar", "activation",
                                 out=out_acc[:, oc, tqb * 512:(tqb + 1) * 512],
                                 in_=pb[bank][:], func=AF.Identity,
                                 bias=bopt[:, oc:oc + 1], inc="sc")
                        P.freed(BK(bank), stok)
                    P.freed("attn_fmb", ("pe", C("pe"), "tensor", False))
                wo_all_done = ("sc", C("sc"), "scalar", False)
                ck(f"attn{s}{th}")
                attn_pe_done = ("pe", C("pe"), "tensor", False)
                P.freed("wstgB", attn_pe_done)
                P.freed("kreg", attn_pe_done)
                P.freed("sqbuf", attn_pe_done)

                for e in range(E):
                    for half in range(2):
                        P.acquire("sync", "cwrow")
                        P.need("sync", cw_done)
                        P.need("sync", attn_pe_done)
                        W("sync", "scw", 16 * ncw[0])
                        ncw[0] += 1
                        I("sync", "dma_start", out=cwrow,
                          in_=cw_fm[e:e + 1, half * 512:(half + 1) * 512],
                          inc="scw", by=16)
                        cwr_tok = ("scw", 16 * ncw[0], "sync", True)
                        P.need("tensor", cwr_tok)
                        P.acquire("tensor", BK(6))
                        mm = I("tensor", "matmul", pb[6][:], onesb[0:1, 0:128],
                               cwrow[0:1, :], start=True, stop=True, inc="pe")
                        P.freed("cwrow", ("pe", C("pe"), "tensor", False))
                        P.need("vector", mm)
                        vtok = I("vector", "tensor_copy",
                                 out=cwb[:, half * 512:(half + 1) * 512],
                                 in_=pb[6][:], inc="dv")
                        P.freed(BK(6), vtok)
                    cwb_tok = ("dv", C("dv"), "vector", False)

                    for hb in range(NHB):
                        f = e * NHB + hb
                        sl = f % 2
                        if sl == 0:
                            P.acquire("gpsimd", "wstgA")
                            P.acquire("gpsimd", "wstgB")
                        else:
                            P.acquire("gpsimd", "kreg")
                        W("gpsimd", S_SW[sl], 16 * nsw[sl])
                        nsw[sl] += 1
                        I("gpsimd", "dma_start", out=w1S[sl][:, :, :],
                          in_=d_w1T[e, :, hb * HB:(hb + 1) * HB].rearrange(
                              "(c p) n -> p c n", p=128), inc=S_SW[sl], by=16)
                        nsw[sl] += 1
                        I("gpsimd", "dma_start", out=w2S[sl][:, :, :],
                          in_=d_w2T[e, hb * HB:(hb + 1) * HB, :].rearrange(
                              "(c p) n -> p c n", p=128), inc=S_SW[sl], by=16)
                        wtok = (S_SW[sl], 16 * nsw[sl], "gpsimd", True)
                        P.need("tensor", wtok)
                        for tb in range(NTB):
                            gtoks = []
                            for hc in range(HCB):
                                hbank = hc % 2
                                P.acquire("tensor", BK(hbank))
                                for ic in range(NIC):
                                    mm = I("tensor", "matmul", pb[hbank][:],
                                           w1S[sl][:, ic, hc * 128:(hc + 1) * 128],
                                           xn_fm[:, ic, qb + tb * TB: qb + (tb + 1) * TB],
                                           start=(ic == 0), stop=(ic == NIC - 1),
                                           inc=("pe" if ic == NIC - 1 else None))
                                P.need("scalar", mm)
                                if hc == 0:
                                    P.acquire("scalar", "h_all")
                                gtok = I("scalar", "activation", out=h_all[:, hc, :],
                                         in_=pb[hbank][:], func=AF.Gelu_apprx_tanh,
                                         bias=b1pt[:, e, hb * HCB + hc: hb * HCB + hc + 1],
                                         inc="sc")
                                gtoks.append(gtok)
                                P.freed(BK(hbank), gtok)
                            P.need("tensor", gtoks[-1])
                            for q in range(4):
                                qb0 = 2 + (q % 3) * 2
                                for hc in range(HCB):
                                    for oc2 in range(2):
                                        oc = q * 2 + oc2
                                        obank = qb0 + oc2
                                        if hc == 0:
                                            P.acquire("tensor", BK(obank))
                                        mm2 = I("tensor", "matmul", pb[obank][:],
                                                w2S[sl][:, hc, oc * 128:(oc + 1) * 128],
                                                h_all[:, hc, :], start=(hc == 0),
                                                stop=(hc == HCB - 1),
                                                inc=("pe" if (hc == HCB - 1 and oc2 == 1)
                                                     else None))
                                P.need("vector", mm2)
                                W("vector", "dv", cwb_tok[1])
                                P.need("vector", wo_all_done)
                                for oc2 in range(2):
                                    vtok = I("vector", "tensor_tensor",
                                             out=pb[qb0 + oc2][:], in0=pb[qb0 + oc2][:],
                                             in1=cwb[:, tb * TB:(tb + 1) * TB],
                                             op=ALU.mult, inc="dv")
                                W("vector", "dv", vtok[1])
                                for oc2 in range(2):
                                    oc = q * 2 + oc2
                                    atok = I("vector", "tensor_tensor",
                                      out=out_acc[:, oc, tb * TB:(tb + 1) * TB],
                                      in0=out_acc[:, oc, tb * TB:(tb + 1) * TB],
                                      in1=pb[qb0 + oc2][:], op=ALU.add,
                                      inc="dv")
                                    P.freed(BK(qb0 + oc2), atok)
                            P.freed("h_all", ("pe", C("pe"), "tensor", False))
                        consumed = ("pe", C("pe"), "tensor", False)
                        if sl == 0:
                            P.freed("wstgA", consumed)
                            P.freed("wstgB", consumed)
                        else:
                            P.freed("kreg", consumed)
                ffn_done = ("dv", C("dv"), "vector", False)
                ck(f"ffn{s}{th}")

                for tc in range(TH // 128):
                    p = tc % 2
                    P.acquire("sync", f"x_stage{p}")
                    P.need("sync", attn_pe_done)
                    W("sync", S_SX[p], 16 * nx[p])
                    nx[p] += 1
                    I("sync", "dma_start", out=x_stage[:, p, :],
                      in_=d_x[s, qb + tc * 128: qb + (tc + 1) * 128, :], inc=S_SX[p], by=16)
                    xtok = (S_SX[p], 16 * nx[p], "sync", True)
                    P.need("tensor", ffn_done)
                    vtok = None
                    for oc in range(NIC):
                        bank = 6 + oc % 2
                        P.acquire("tensor", BK(bank))
                        ttok = I("tensor", "transpose", pb[bank][:, 0:128],
                                 out_acc[:, oc, tc * 128:(tc + 1) * 128], identf[:],
                                 inc="pe")
                        P.need("vector", ttok)
                        P.need("vector", xtok)
                        if oc == 0:
                            P.acquire("vector", f"out_tm{p}")
                        vtok = I("vector", "tensor_tensor",
                                 out=out_tm[:, p, oc * 128:(oc + 1) * 128],
                                 in0=pb[bank][:, 0:128],
                                 in1=x_stage[:, p, oc * 128:(oc + 1) * 128],
                                 op=ALU.add, inc="dv")
                        P.freed(BK(bank), vtok)
                        if oc == NIC - 1:
                            P.freed(f"x_stage{p}", vtok)
                    P.need("sync", vtok)
                    W("sync", S_ST[p], 16 * nst[p])
                    nst[p] += 1
                    I("sync", "dma_start",
                      out=d_out[s, qb + tc * 128: qb + (tc + 1) * 128, :],
                      in_=out_tm[:, p, :], inc=S_ST[p], by=16)
                    P.freed(f"out_tm{p}", (S_ST[p], 16 * nst[p], "sync", True))
                P.freed("out_acc_w", ("pe", C("pe"), "tensor", False))
                ck(f"final{s}{th}")

        except _StopPlan:
            pass

        if upto is not None:
            for sname in ["pe", "sc", "dv", "sg"]:
                W("sync", sname, C(sname))
            dumps = [("dbg_xn", xn_fm[:, :, :], [128, NIC, S], BF16),
                     ("dbg_q", q_fm, [128, NIC, TH], BF16),
                     ("dbg_k", k_fm, [128, NIC, S], BF16),
                     ("dbg_v", v_tm, [128, S // 128, H * 65], BF16),
                     ("dbg_cw", cw_fm[:, :], [E, TH], BF16),
                     ("dbg_acc", out_acc[:, :, :], [128, NIC, TH], F32)]
            for nm, src, shp, dt in dumps:
                dd = nc.dram_tensor(nm, shp, dt, kind="ExternalOutput").ap()
                nst[0] += 1
                I("sync", "dma_start", out=dd, in_=src, inc=S_ST[0], by=16)

        W("sync", S_ST[0], 16 * nst[0])
        W("sync", S_ST[1], 16 * nst[1])

        for e_ in ENGS:
            for op in P.ops[e_]:
                if op[0] == "w":
                    assert op[2] <= C(op[1]), (e_, op, C(op[1]))

        slopes = _slopes(H)

        def replay(eng_name, eng):
            for op in P.ops[eng_name]:
                if op[0] == "w":
                    eng.wait_ge(sems[op[1]], op[2])
                else:
                    _, method, args, kw, inc, by = op
                    kw2 = {}
                    for k, v in kw.items():
                        if isinstance(v, tuple) and len(v) == 2 and v[0] == "NEGSLOPE":
                            v = float(-slopes[v[1]])
                        kw2[k] = v
                    inst = getattr(eng, method)(*args, **kw2)
                    if inc:
                        inst.then_inc(sems[inc], by)

        block = es.enter_context(nc.Block())

        @block.sync
        def _(eng):
            replay("sync", eng)

        @block.gpsimd
        def _(eng):
            replay("gpsimd", eng)

        @block.tensor
        def _(eng):
            replay("tensor", eng)

        @block.scalar
        def _(eng):
            replay("scalar", eng)

        @block.vector
        def _(eng):
            replay("vector", eng)

    return nc


def _prep_inputs(ins):
    f = np.float32
    slopes = _slopes(H)
    wqT = np.ascontiguousarray(ins["L0_attn_Wq"].T)
    wkT = np.ascontiguousarray(ins["L0_attn_Wk"].T)
    wvT = np.ascontiguousarray(ins["L0_attn_Wv"].T)
    woT = np.ascontiguousarray(ins["L0_attn_Wo"].T)
    w1T = np.ascontiguousarray(ins["L0_ffn_w1"].transpose(0, 2, 1))
    w2T = np.ascontiguousarray(ins["L0_ffn_w2"].transpose(0, 2, 1))
    rwT = np.ascontiguousarray(ins["L0_ffn_router_w"].T)
    bqpt = np.ascontiguousarray(ins["L0_attn_bq"].reshape(NIC, 128).T)
    bkpt = np.ascontiguousarray(ins["L0_attn_bk"].reshape(NIC, 128).T)
    bopt = np.ascontiguousarray(ins["L0_attn_bo"].reshape(NIC, 128).T)
    bvr = np.ascontiguousarray(ins["L0_attn_bv"].reshape(1, D))
    b1pt = np.ascontiguousarray(
        ins["L0_ffn_b1"].reshape(E, HID // 128, 128).transpose(2, 0, 1))
    b2 = np.ascontiguousarray(ins["L0_ffn_b2"])
    identb = np.eye(128).astype(ml_dtypes.bfloat16)
    identf = np.eye(128, dtype=f)
    jj, pp = np.meshgrid(np.arange(S // 128, dtype=f), np.arange(128, dtype=f))
    rowshift = np.zeros((128, H * (S // 128)), f)
    for h in range(H):
        rowshift[:, h * 16:(h + 1) * 16] = slopes[h] * (128.0 * jj + pp)
    minf = np.tril(np.full((128, 128), -1e30, f), -1)
    tqrow = np.arange(S, dtype=f).reshape(1, S)
    shared = dict(wqT=wqT, wkT=wkT, wvT=wvT, woT=woT, w1T=w1T, w2T=w2T, rwT=rwT,
                  bqpt=bqpt, bkpt=bkpt, bopt=bopt, bvr=bvr, b1pt=b1pt, b2=b2,
                  identb=identb, identf=identf, rowshift=rowshift, minf=minf,
                  tqrow=tqrow)
    x = np.ascontiguousarray(ins["L0_attn_x"])
    in_maps = []
    for c in range(NCORE):
        m = dict(shared)
        m["x"] = np.ascontiguousarray(x[c * BPC:(c + 1) * BPC])
        in_maps.append(m)
    return in_maps


def kernel(**ins):
    nc = build_core_program()
    in_maps = _prep_inputs(ins)
    res = run_bass_kernel_spmd(nc, in_maps, core_ids=list(range(NCORE)))
    return np.concatenate([r["out"] for r in res.results], axis=0)
